# version 21
# baseline (speedup 1.0000x reference)
import numpy as np
import ml_dtypes
import concourse.bass as bass
import concourse.mybir as mybir
from concourse.bass_utils import run_bass_kernel_spmd
from contextlib import ExitStack
import bisect

F32 = mybir.dt.float32
BF16 = mybir.dt.bfloat16
U8 = mybir.dt.uint8
AF = mybir.ActivationFunctionType
ALU = mybir.AluOpType
AX = mybir.AxisListType

NT = 4096
D = 1024
NKC = 8
WIN = 512
NW = NT // WIN
HEADS = 8
DH = 64
HID = 2730
PAIRS = 22
NTC = 8
BF = ml_dtypes.bfloat16

ENGINES = ("pe", "act", "dve", "pool", "sp")
QUEUES = ("sp", "act", "pool")


def _host_consts():
    c = {}
    c["ident_bf"] = np.eye(128, dtype=BF)
    c["ident_f32"] = np.eye(128, dtype=np.float32)
    P = np.zeros((128, 128), np.float32)
    for blk in (0, 64):
        for d in range(32):
            P[blk + d + 32, blk + d] = -1.0
            P[blk + d, blk + d + 32] = 1.0
    c["pmat"] = P.astype(BF)
    ones2 = np.zeros((128, 2), np.float32)
    ones2[0:64, 0] = 1.0
    ones2[64:128, 1] = 1.0
    c["ones2_f32"] = ones2
    c["ones2t_f32"] = np.ascontiguousarray(ones2.T)
    c["ones_row_f32"] = np.ones((1, 128), np.float32)
    o84 = np.zeros((128, 32), np.float32)
    for kt in range(4):
        for p in range(128):
            o84[p, kt * 8 + 2 * kt + p // 64] = 1.0
    c["ones8x4"] = o84.astype(BF)
    sq8 = np.zeros((8, 512), np.float32)
    for kt in range(4):
        for i in range(128):
            sq8[2 * kt + i // 64, kt * 128 + i] = 1.0
    c["selq8"] = sq8
    cs8 = np.zeros((128, 64), np.float32)
    for tc in range(8):
        cs8[:, tc * 8 + tc] = 1.0
    c["colsel8"] = cs8.astype(BF)
    sr8 = np.zeros((8, 1024), np.float32)
    for tc in range(8):
        sr8[tc, tc * 128:(tc + 1) * 128] = 1.0
    c["selrow8"] = sr8
    c["ones_col_bf"] = np.ones((128, 1), BF)
    half = np.arange(0, DH, 2, dtype=np.float64)
    inv_freq = 1.0 / 10000.0 ** (half / DH)
    t = np.arange(NT, dtype=np.float64)
    freqs = t[:, None] * inv_freq[None, :]
    freqs = np.concatenate([freqs, freqs], axis=-1)
    base = (half + 0.4 * DH) / (1.4 * DH)
    power = (t - NT // 2) / float(WIN // 2)
    scale = base[None, :] ** power[:, None]
    scale = np.concatenate([scale, scale], axis=-1)
    cos, sin = np.cos(freqs), np.sin(freqs)
    dup = lambda a: np.concatenate([a, a], axis=0).astype(np.float32)
    c["tq_a"] = dup((cos * scale).T)
    c["tq_b"] = dup((sin * scale).T)
    c["tk_a"] = dup((cos / scale).T)
    c["tk_b"] = dup((sin / scale).T)
    jj = np.arange(128)[:, None]
    ii = np.arange(384)[None, :]
    c["mask_band"] = (((jj <= ii) & (ii <= jj + 255)).astype(np.float32) / 512.0)
    r = np.arange(128)[None, :]
    jj1 = np.arange(128)[:, None]
    c["tri_past"] = np.where(jj1 >= r, 1.0e30, -1.0e30).astype(np.float32)
    c["tri_fut"] = np.where(jj1 <= r, 1.0e30, -1.0e30).astype(np.float32)
    c["eps_col"] = np.full((128, 1), 1.0e-5, np.float32)
    return c


def _pack_params(inputs):
    p = {}
    p["x_full"] = np.ascontiguousarray(inputs["L0_attn_x"], dtype=np.float32)
    p["W_qkv"] = np.ascontiguousarray(inputs["L0_attn_W_qkv"], dtype=np.float32)
    bq = np.asarray(inputs["L0_attn_b_qkv"], dtype=np.float32)
    p["bqkv_cols"] = np.ascontiguousarray(bq[0:2048].reshape(16, 128).T)
    p["bqv_row"] = np.ascontiguousarray(bq[2048:3072].reshape(1, 1024))
    p["lnw_row"] = np.asarray(inputs["L0_attn_ln_w"], np.float32).reshape(1, D)
    p["lnb_row"] = np.asarray(inputs["L0_attn_ln_b"], np.float32).reshape(1, D)
    p["wqkv"] = np.ascontiguousarray(inputs["L0_ffn_wqkv"], dtype=np.float32)
    p["wo"] = np.ascontiguousarray(inputs["L0_ffn_wo"], dtype=np.float32)
    p["w1"] = np.ascontiguousarray(inputs["L0_ffn_w1"], dtype=np.float32)
    p["w2"] = np.ascontiguousarray(inputs["L0_ffn_w2"], dtype=np.float32)
    qs = np.asarray(inputs["L0_ffn_q_scale"], np.float32)
    ks = np.asarray(inputs["L0_ffn_k_scale"], np.float32)
    p["qs_dup"] = np.ascontiguousarray(np.concatenate([qs, qs], 1).reshape(2, 128, 1))
    p["ks_dup"] = np.ascontiguousarray(np.concatenate([ks, ks], 1).reshape(2, 128, 1))
    p["qs_nat"] = qs.reshape(2, 1, 64).copy()
    p["ks_nat"] = ks.reshape(2, 1, 64).copy()
    lg = np.asarray(inputs["L0_ffn_ln_g"], np.float32)
    lb = np.asarray(inputs["L0_ffn_ln_b"], np.float32)
    p["lg_cols"] = np.ascontiguousarray(lg.reshape(2, 8, 128).transpose(0, 2, 1))
    p["lb_cols"] = np.ascontiguousarray(lb.reshape(2, 8, 128).transpose(0, 2, 1))
    b1 = np.asarray(inputs["L0_ffn_b1"], np.float32)
    b1a = np.zeros((2, 128, PAIRS), np.float32)
    b1g = np.zeros((2, 128, PAIRS), np.float32)
    for k in range(PAIRS):
        w = min(128, HID - k * 128)
        b1a[:, 0:w, k] = b1[:, k * 128:k * 128 + w]
        b1g[:, 0:w, k] = b1[:, HID + k * 128:HID + k * 128 + w]
    p["b1a_cols"] = b1a
    p["b1g_cols"] = b1g
    b2 = np.asarray(inputs["L0_ffn_b2"], np.float32)
    p["b2_cols"] = np.ascontiguousarray(b2.reshape(2, 8, 128).transpose(0, 2, 1))
    return p


class Buf:
    __slots__ = ("name", "writer", "readers", "psum")

    def __init__(self, name):
        self.name = name
        self.writer = None
        self.readers = []
        self.psum = False


class Step:
    __slots__ = ("eng", "fn", "waits", "incs", "idx", "qtok")

    def __init__(self, eng, fn, idx):
        self.eng = eng
        self.fn = fn
        self.waits = []
        self.incs = []
        self.idx = idx
        self.qtok = None


SERIAL = False
DEBUG_BARS = False
EN_UPTO = 99
PREFIX_N = None


class Sched:
    def __init__(self, nc, es):
        self.nc = nc
        self.es = es
        self.sem = {e: es.enter_context(nc.semaphore(f"sem_{e}")) for e in ENGINES}
        self.dsem = {}
        self.steps = {e: [] for e in ENGINES}
        self.cnt = {e: 0 for e in ENGINES}
        self.tick_idx = {e: [] for e in ENGINES}
        self.tick_val = {e: [] for e in ENGINES}
        self.hw = {e: {} for e in ENGINES}
        self.pending_bar = {e: [] for e in ENGINES}
        self.n_steps = 0

    def _get_dsem(self, key):
        if key not in self.dsem:
            sem = self.es.enter_context(self.nc.semaphore(f"dsem{len(self.dsem)}"))
            self.dsem[key] = [sem, 0]
        return self.dsem[key]

    def _step_token(self, st):
        if st.qtok is not None:
            return st.qtok
        e = st.eng
        i = bisect.bisect_left(self.tick_idx[e], st.idx)
        if i < len(self.tick_idx[e]):
            return ("e", e, self.tick_val[e][i])
        assert not st.incs
        self.cnt[e] += 1
        v = self.cnt[e]
        st.incs.append((self.sem[e], 1))
        self.tick_idx[e].append(st.idx)
        self.tick_val[e].append(v)
        return ("e", e, v)

    def _add_wait(self, step, tok, strict=False):
        if tok is None:
            return
        if isinstance(tok, Step):
            if tok.eng == step.eng and not strict and step.eng == "pe":
                return
            tok = self._step_token(tok)
        kind, key, val = tok
        if kind == "e" and key == step.eng and step.eng == "pe" and not strict:
            return
        hw = self.hw[step.eng]
        if hw.get((kind, key), -1) >= val:
            return
        hw[(kind, key)] = val
        sem = self.sem[key] if kind == "e" else self.dsem[key][0]
        step.waits.append((sem, val))

    def op(self, eng, fn, reads=(), writes=(), strict=False):
        st = Step(eng, fn, self.n_steps)
        self.n_steps += 1
        if SERIAL:
            for e2 in ENGINES:
                if e2 != eng:
                    for s2 in reversed(self.steps[e2]):
                        if s2.qtok is None:
                            self._add_wait(st, self._step_token(s2), strict=True)
                            break
            for key, (sem, cnt) in list(self.dsem.items()):
                if cnt:
                    self._add_wait(st, ("d", key, cnt), strict=True)
        for w in self.pending_bar[eng]:
            self._add_wait(st, w, strict=True)
        self.pending_bar[eng] = []
        for b in reads:
            self._add_wait(st, b.writer, strict=strict)
            if b.psum:
                for r in b.readers:
                    if isinstance(r, Step) and r.eng != eng:
                        self._add_wait(st, r, strict=True)
        for b in writes:
            self._add_wait(st, b.writer, strict=strict)
            for r in b.readers:
                self._add_wait(st, r, strict=strict)
        self.steps[eng].append(st)
        for b in writes:
            b.writer = st
            b.readers = []
        for b in reads:
            b.readers.append(st)
        return st

    def dma(self, q, out_ap, in_ap, reads=(), writes=(), sem_key=None, **kw):
        if sem_key is None:
            if writes:
                sem_key = writes[0]
            elif reads:
                sem_key = reads[0]
            else:
                sem_key = ("fq", q)
        ds = self._get_dsem(sem_key)
        ds[1] += 16
        tok = ("d", sem_key, ds[1])
        qs = ds[0]

        def fn(eng, out_ap=out_ap, in_ap=in_ap, kw=kw):
            return eng.dma_start(out=out_ap, in_=in_ap, **kw)

        st = self.op(q, fn, reads=reads, writes=writes)
        st.incs.append((qs, 16))
        st.qtok = tok
        for b in writes:
            b.writer = tok
        for b in reads:
            if b.readers and b.readers[-1] is st:
                b.readers[-1] = tok
            else:
                b.readers.append(tok)
        return tok

    def barrier(self):
        toks = []
        for e in ENGINES:
            for s2 in reversed(self.steps[e]):
                if s2.qtok is None:
                    toks.append(self._step_token(s2))
                    break
        for key, (sem, cnt) in self.dsem.items():
            if cnt:
                toks.append(("d", key, cnt))
        for e in ENGINES:
            self.pending_bar[e] = list(toks)

    def finish(self):
        for key, (sem, cnt) in self.dsem.items():
            if cnt:
                st = Step("sp", lambda eng, sem=sem, qv=cnt: eng.wait_ge(sem, qv),
                          self.n_steps)
                self.n_steps += 1
                self.steps["sp"].append(st)

    def replay(self, block):
        lim = PREFIX_N
        drain = {}
        if lim is not None:
            for e in ENGINES:
                for st in self.steps[e]:
                    if st.idx < lim and st.qtok is not None:
                        kind, key, val = st.qtok
                        drain[key] = max(drain.get(key, 0), val)

        def section(e):
            def run(eng):
                for st in self.steps[e]:
                    if lim is not None and st.idx >= lim:
                        continue
                    for (sem, val) in st.waits:
                        eng.wait_ge(sem, val)
                    inst = st.fn(eng)
                    for (sem, n) in st.incs:
                        inst.then_inc(sem, n)
                if e == "sp" and lim is not None:
                    for key, val in drain.items():
                        eng.wait_ge(self.dsem[key][0], val)
            return run
        block.tensor(section("pe"))
        block.scalar(section("act"))
        block.vector(section("dve"))
        block.gpsimd(section("pool"))
        block.sync(section("sp"))


def build_program(taps=(), stages=("p1", "p2", "p3")):
    taps = set(taps)
    stages = set(stages)
    nc = bass.Bass("TRN2", target_bir_lowering=False)
    es = ExitStack()
    tap_outs = []

    dr = {}
    dr["x"] = nc.dram_tensor("x", [NT, D], F32, kind="ExternalInput").ap()
    dr["out"] = nc.dram_tensor("out", [NT, D], F32, kind="ExternalOutput").ap()
    for nm, shp, dt in [
        ("W_qkv", [D, 3 * D], F32), ("bqkv_cols", [128, 16], F32),
        ("bqv_row", [1, D], F32), ("lnw_row", [1, D], F32), ("lnb_row", [1, D], F32),
        ("wqkv", [2, D, 1536], F32), ("wo", [2, 512, D], F32),
        ("w1", [2, D, 5460], F32), ("w2", [2, HID, D], F32),
        ("qs_dup", [2, 128, 1], F32), ("ks_dup", [2, 128, 1], F32),
        ("qs_nat", [2, 1, 64], F32), ("ks_nat", [2, 1, 64], F32),
        ("lg_cols", [2, 128, 8], F32), ("lb_cols", [2, 128, 8], F32),
        ("b1a_cols", [2, 128, PAIRS], F32), ("b1g_cols", [2, 128, PAIRS], F32),
        ("b2_cols", [2, 128, 8], F32),
        ("ident_bf", [128, 128], BF16), ("ident_f32", [128, 128], F32),
        ("pmat", [128, 128], BF16),
        ("ones2_f32", [128, 2], F32), ("ones2t_f32", [2, 128], F32),
        ("ones_row_f32", [1, 128], F32), ("ones_col_bf", [128, 1], BF16),
        ("ones8x4", [128, 32], BF16), ("selq8", [8, 512], F32),
        ("colsel8", [128, 64], BF16), ("selrow8", [8, 1024], F32),
        ("tq_a", [128, NT], F32), ("tq_b", [128, NT], F32),
        ("tk_a", [128, NT], F32), ("tk_b", [128, NT], F32),
        ("mask_band", [128, 384], F32), ("tri_past", [128, 128], F32),
        ("tri_fut", [128, 128], F32),
        ("eps_col", [128, 1], F32),
    ]:
        dr[nm] = nc.dram_tensor(nm, shp, dt, kind="ExternalInput").ap()
    dr["w1s"] = nc.dram_tensor("w1s", [2, D, 5460], BF16, kind="Internal").ap()
    dr["w2s"] = nc.dram_tensor("w2s", [2, HID, D], BF16, kind="Internal").ap()
    dr["wqs"] = nc.dram_tensor("wqs", [D, 3 * D], BF16, kind="Internal").ap()
    dr["xnTd"] = nc.dram_tensor("xnTd", [NKC, 128, NT], BF16, kind="Internal").ap()
    dr["wqks"] = nc.dram_tensor("wqks", [2, D, 1536], BF16, kind="Internal").ap()

    S = Sched(nc, es)

    def sb(name, shape, dtype, scope=es):
        t = scope.enter_context(nc.sbuf_tensor(name, shape, dtype))
        return t, Buf(name)

    def psum(name, shape, dtype, scope):
        t = scope.enter_context(nc.psum_tensor(name, shape, dtype))
        b = Buf(name)
        b.psum = True
        return t, b

    dbuf = {k: Buf(k) for k in dr}

    def tap(name, ap, reads):
        if name in taps:
            td = nc.dram_tensor(f"tap_{name}", list(ap.shape), ap.dtype,
                                kind="ExternalOutput").ap()
            S.dma("sp", td, ap, reads=reads, sem_key=("tap", name))
            tap_outs.append(f"tap_{name}")

    yt, ytb = [], []
    for kc in range(NKC):
        t, b = sb(f"y{kc}", [128, 1 + NT], BF16)
        yt.append(t)
        ytb.append(b)

    consts = {}
    for nm, shp, dt in [
        ("ident_bf", [128, 128], BF16), ("ident_f32", [128, 128], F32),
        ("pmat", [128, 128], BF16), ("ones2_f32", [128, 2], F32),
        ("ones2t_f32", [2, 128], F32), ("ones_row_f32", [1, 128], F32),
        ("ones_col_bf", [128, 1], BF16),
        ("tri_past", [128, 128], F32), ("tri_fut", [128, 128], F32),
        ("eps_col", [128, 1], F32),
        ("ones8x4", [128, 32], BF16), ("selq8", [8, 512], F32),
        ("colsel8", [128, 64], BF16), ("selrow8", [8, 1024], F32),
    ]:
        t, b = sb("c_" + nm, shp, dt)
        consts[nm] = (t, b)
        S.dma("pool", t[:], dr[nm], reads=[dbuf[nm]], writes=[b])

    IDB = consts["ident_bf"][0]
    IDF = consts["ident_f32"][0]

    for l in range(2):
        for kc in range(NKC):
            S.dma("pool", dr["w1s"][l, kc * 128:(kc + 1) * 128, :],
                  dr["w1"][l, kc * 128:(kc + 1) * 128, :],
                  reads=[dbuf["w1"]], sem_key="prep")
            S.dma("pool", dr["wqks"][l, kc * 128:(kc + 1) * 128, :],
                  dr["wqkv"][l, kc * 128:(kc + 1) * 128, :],
                  reads=[dbuf["wqkv"]], sem_key="prep")
        for fk in range(6):
            a0, a1 = fk * 512, min(HID, (fk + 1) * 512)
            S.dma("pool", dr["w2s"][l, a0:a1, :], dr["w2"][l, a0:a1, :],
                  reads=[dbuf["w2"]], sem_key="prep")
    for kc in range(NKC):
        S.dma("pool", dr["wqs"][kc * 128:(kc + 1) * 128, :],
              dr["W_qkv"][kc * 128:(kc + 1) * 128, :],
              reads=[dbuf["W_qkv"]], sem_key="prep")
    prep_tok = ("d", "prep", S.dsem["prep"][1])
    for bname in ("w1s", "wqks", "w2s", "wqs"):
        dbuf[bname].writer = prep_tok

    for kc in range(NKC):
        S.op("dve", lambda e, t=yt[kc]: e.memset(t[:, 0:1], 0.0), writes=[ytb[kc]])

    p1 = ExitStack()
    ps_trb, ps_trb_b = psum("ps_trb", [128, 1024], BF16, p1)
    x_sb = [sb(f"p1x{i}", [128, D], F32, p1) for i in range(4)]
    xn_sb = [sb(f"p1xn{i}", [128, D], BF16, p1) for i in range(4)]
    junk = sb("p1junk", [128, D], BF16, p1)
    stat = [{nm: sb(f"p1st{nm}{i}", [128, 1], F32, p1)
             for nm in ("s1", "s2", "mean", "m2", "var", "sd", "rstd")}
            for i in range(2)]

    def emit_ln_tile(xt, xtb, dst, dstb, stt, jt, jtb):
        S.op("dve", lambda e, a=xt, o=stt["s1"]: e.reduce_sum(
            out=o[0][:], in_=a[:], axis=AX.X),
            reads=[xtb], writes=[stt["s1"][1]])
        S.op("act", lambda e, a=xt, o=stt["s2"], j=jt: e.activation(
            out=j[:], in_=a[:], func=AF.Square, accum_out=o[0][:]),
            reads=[xtb], writes=[stt["s2"][1], jtb])
        S.op("act", lambda e, i=stt["s1"], o=stt["mean"]: e.mul(
            out=o[0][:], in_=i[0][:], mul=1.0 / D),
            reads=[stt["s1"][1]], writes=[stt["mean"][1]], strict=True)
        S.op("act", lambda e, i=stt["mean"], o=stt["m2"]: e.activation(
            out=o[0][:], in_=i[0][:], func=AF.Square),
            reads=[stt["mean"][1]], writes=[stt["m2"][1]], strict=True)
        S.op("dve", lambda e, i=stt["s2"], m2=stt["m2"], o=stt["var"]: e.scalar_tensor_tensor(
            out=o[0][:], in0=i[0][:], scalar=1.0 / D, in1=m2[0][:],
            op0=ALU.mult, op1=ALU.subtract),
            reads=[stt["s2"][1], stt["m2"][1]], writes=[stt["var"][1]], strict=True)
        S.op("act", lambda e, i=stt["var"], o=stt["sd"]: e.activation(
            out=o[0][:], in_=i[0][:], func=AF.Sqrt, bias=consts["eps_col"][0][:]),
            reads=[stt["var"][1], consts["eps_col"][1]], writes=[stt["sd"][1]], strict=True)
        S.op("dve", lambda e, i=stt["sd"], o=stt["rstd"]: e.reciprocal(
            out=o[0][:], in_=i[0][:]),
            reads=[stt["sd"][1]], writes=[stt["rstd"][1]], strict=True)
        S.op("dve", lambda e, a=xt, o=dst, m=stt["mean"], r=stt["rstd"]: e.tensor_scalar(
            out=o[:], in0=a[:], scalar1=m[0][:], scalar2=r[0][:], op0=ALU.subtract,
            op1=ALU.mult),
            reads=[xtb, stt["mean"][1], stt["rstd"][1]], writes=[dstb], strict=True)

    for it in (range(32) if "p1" in stages else []):
        sl = it % 4
        xt, xtb = x_sb[sl]
        S.dma("sp", xt[:], dr["x"][it * 128:(it + 1) * 128, :],
              reads=[dbuf["x"]], writes=[xtb])
        xnt, xnb = xn_sb[sl]
        emit_ln_tile(xt, xtb, xnt, xnb, stat[it % 2], junk[0], junk[1])
        for kc in range(NKC):
            S.op("pe", lambda e, src=xnt, kc=kc: e.transpose(
                out=ps_trb[:, kc * 128:(kc + 1) * 128],
                in_=src[:, kc * 128:(kc + 1) * 128], identity=IDB[:]),
                reads=[xnb, consts["ident_bf"][1]], writes=[ps_trb_b])
        for kc in range(NKC):
            S.op("dve", lambda e, kc=kc, it=it: e.tensor_copy(
                out=yt[kc][:, 1 + it * 128:1 + (it + 1) * 128],
                in_=ps_trb[:, kc * 128:(kc + 1) * 128]),
                reads=[ps_trb_b], writes=[ytb[kc]])
        if it == 0:
            tap("xn_t0", xn_sb[0][0][:], [xn_sb[0][1]])
    tap("y_p1", yt[0][:, 1:513], [ytb[0]])
    if "p1" in stages:
        for kc in range(NKC):
            S.dma("pool", dr["xnTd"][kc], yt[kc][:, 1:1 + NT],
                  reads=[ytb[kc]], sem_key="xnspill")
        dbuf["xnTd"].writer = ("d", "xnspill", S.dsem["xnspill"][1])
    p1.close()
    S.barrier()

    p2 = ExitStack()
    ps_f, ps_fb = [], []
    for i in range(8):
        t, b = psum(f"ps_f{i}", [128, 512], F32, p2)
        ps_f.append(t)
        ps_fb.append(b)

    wo_sb = [sb(f"wo{kt}", [128, 1024], BF16, p2) for kt in range(4)]
    qs_col = sb("qs_col", [128, 1], F32, p2)
    ks_col = sb("ks_col", [128, 1], F32, p2)
    qsn = sb("qsn", [1, 64], F32, p2)
    ksn = sb("ksn", [1, 64], F32, p2)
    lg_col = sb("lg_col", [128, 8], F32, p2)
    lb_col = sb("lb_col", [128, 8], F32, p2)
    b1a_col = sb("b1a_col", [128, PAIRS], F32, p2)
    b1g_col = sb("b1g_col", [128, PAIRS], F32, p2)
    b2_col = sb("b2_col", [128, 8], F32, p2)
    neg_m = sb("neg_m", [128, 1], F32, p2)
    mrow = sb("mrow", [1, 1], F32, p2)
    mrow2 = sb("mrow2", [1, 1], F32, p2)
    qkslab = [sb(f"qksl{i}", [128, 256], BF16, p2) for i in range(4)]
    vslab2 = [sb(f"vsl2{i}", [128, 512], BF16, p2) for i in range(4)]
    qTt2 = [[sb(f"qT{par}_{kt}", [128, 512], BF16, p2) for kt in range(4)]
            for par in range(2)]
    kTs = [[sb(f"kT{s}_{kt}", [128, 512], BF16, p2) for kt in range(4)] for s in range(3)]
    v_aug = [sb(f"vaug{s}", [128, 8 * 65], BF16, p2) for s in range(12)]
    sq_sb = sb("sq_sb", [128, 512], BF16, p2)
    qn8 = sb("qn8", [8, 512], F32, p2)
    kn8 = sb("kn8", [8, 512], F32, p2)
    rk_all = sb("rk_all", [128, 32 * 8], F32, p2)
    shufqs = [sb("shufq0", [128, 512], BF16, p2)] * 2
    tmp1s = [sb("tmp10", [128, 512], F32, p2)] * 2
    tmp2s = [sb("tmp20", [128, 512], F32, p2)] * 2
    tmp1 = tmp1s[0]
    tabs = [sb(f"tab{i}", [128, 512], F32, p2) for i in range(4)]
    pT = [sb(f"pT{jt}", [128, 512], BF16, p2) for jt in range(8)]
    o_asm = [sb(f"oasm{kt}", [128, 512], BF16, p2) for kt in range(4)]
    rl = [sb(f"rl{i}", [1, 512], F32, p2) for i in range(2)]
    h2n = [sb(f"h2n{kc}", [128, 512], BF16, p2) for kc in range(NKC)]
    csq, csq_b = sb("csq", [128, 512], BF16, p2)
    stat8a = sb("stat8a", [8, 512], F32, p2)
    stat8b = sb("stat8b", [8, 512], F32, p2)
    a_bc = sb("a_bc", [128, 512], F32, p2)
    b_bc = sb("b_bc", [128, 512], F32, p2)
    gel = sb("gel", [128, 512], BF16, p2)
    ffn_sb = [sb(f"ffn{k}", [128, 512], BF16, p2) for k in range(PAIRS)]
    w1sl = [sb(f"w1sl{i}", [128, 8 * 128], BF16, p2) for i in range(4)]
    w2sl = [sb(f"w2sl{i}", [128, 1024], BF16, p2) for i in range(2)]

    for s in range(12):
        t, b = v_aug[s]
        S.op("dve", lambda e, t=t: e.memset(
            t[:].rearrange("p (g c) -> p g c", c=65)[:, :, 64:65], 1.0), writes=[b])
    for jt in range(8):
        t, b = pT[jt]
        S.op("dve", lambda e, t=t: e.memset(t[:], 0.0), writes=[b])
    S.op("dve", lambda e: e.memset(rk_all[0][:], 0.0), writes=[rk_all[1]])

    def hv(kc, t0, n):
        if kc < 4:
            return yt[kc][:, 1 + t0:1 + t0 + n]
        return yt[kc][:, t0:t0 + n]

    JT_A = [max(0, jt * 128 - 512) for jt in range(8)]
    JT_B = [min(512, jt * 128 + 128) for jt in range(8)]
    QK_GROUPS = [(0, 1), (2, 3), (4, 5), (6, 7)]

    def emit_norms_and_evac(l, m, w, bank, bankb):
        t0 = w * WIN
        is_q = m < 4
        kt = m if is_q else m - 4
        shufq = shufqs[m % 2]
        tmp1 = tmp1s[m % 2]
        tmp2 = tmp2s[m % 2]
        S.op("act", lambda e, bank=bank: e.activation(
            out=sq_sb[0][:], in_=bank[:], func=AF.Square),
            reads=[bankb], writes=[sq_sb[1]])
        S.op("pe", lambda e, kt=kt: e.matmul(
            out=ps_f[2][0:8, :], lhsT=consts["ones8x4"][0][:, kt * 8:(kt + 1) * 8],
            rhs=sq_sb[0][:], start=(kt == 0), stop=(kt == 3)),
            reads=[sq_sb[1], consts["ones8x4"][1]], writes=[ps_fb[2]])
        if is_q:
            dst, dstb = qTt2[w % 2][kt]
            scl = qs_col
        else:
            dst, dstb = kTs[w % 3][kt]
            scl = ks_col
        S.op("act", lambda e, bank=bank, dst=dst, scl=scl: e.activation(
            out=dst[:], in_=bank[:], func=AF.Copy, scale=scl[0][:]),
            reads=[bankb, scl[1]], writes=[dstb], strict=True)
        if l == 0 and w == 0 and m == 0:
            tap("q0pre", dst[:], [dstb])
        ta, tab_ = tabs[0 if is_q else 2]
        tb, tbb = tabs[1 if is_q else 3]
        S.op("pe", lambda e, dst=dst, dstb=dstb: e.matmul(
            out=ps_f[3][:], lhsT=consts["pmat"][0][:], rhs=dst[:],
            start=True, stop=True),
            reads=[dstb, consts["pmat"][1]], writes=[ps_fb[3]])
        S.op("act", lambda e: e.activation(out=shufq[0][:], in_=ps_f[3][:], func=AF.Copy),
             reads=[ps_fb[3]], writes=[shufq[1]])
        S.op("pool", lambda e, dst=dst, ta=ta: e.tensor_mul(
            out=tmp1[0][:], in0=dst[:], in1=ta[:]),
            reads=[dstb, tab_], writes=[tmp1[1]])
        S.op("pool", lambda e, tb=tb: e.tensor_mul(
            out=tmp2[0][:], in0=shufq[0][:], in1=tb[:]),
            reads=[shufq[1], tbb], writes=[tmp2[1]])
        S.op("dve", lambda e, dst=dst: e.tensor_add(
            out=dst[:], in0=tmp1[0][:], in1=tmp2[0][:]),
            reads=[tmp1[1], tmp2[1]], writes=[dstb])
        if l == 0 and w == 0 and m == 4:
            tap("k0", dst[:], [dstb])

    def emit_q_chain(l, w):
        S.op("dve", lambda e: e.tensor_scalar_max(
            out=qn8[0][:], in0=ps_f[2][0:8, :], scalar1=1.0e-24),
            reads=[ps_fb[2]], writes=[qn8[1]])
        S.op("act", lambda e: e.activation(out=qn8[0][:], in_=qn8[0][:], func=AF.Ln),
             reads=[qn8[1]], writes=[qn8[1]])
        S.op("act", lambda e: e.activation(out=qn8[0][:], in_=qn8[0][:], func=AF.Exp,
                                           scale=-0.5),
             reads=[qn8[1]], writes=[qn8[1]], strict=True)
        for kt in range(4):
            bx, bxb = (ps_f[2], ps_fb[2]) if kt % 2 == 0 else (ps_f[3], ps_fb[3])
            S.op("pe", lambda e, kt=kt, bx=bx: e.matmul(
                out=bx[:], lhsT=consts["selq8"][0][:, kt * 128:(kt + 1) * 128],
                rhs=qn8[0][:], start=True, stop=True),
                reads=[qn8[1], consts["selq8"][1]], writes=[bxb])
            dst, dstb = qTt2[w % 2][kt]
            S.op("dve", lambda e, dst=dst, bx=bx: e.tensor_tensor(
                out=dst[:], in0=dst[:], in1=bx[:], op=ALU.mult),
                reads=[dstb, bxb], writes=[dstb])
            if l == 0 and w == 0 and kt == 0:
                tap("q0", dst[:], [dstb])

    def emit_k_chain(l, w):
        t0 = w * WIN
        S.op("dve", lambda e: e.tensor_scalar_max(
            out=kn8[0][:], in0=ps_f[2][0:8, :], scalar1=1.0e-24),
            reads=[ps_fb[2]], writes=[kn8[1]])
        S.op("act", lambda e: e.activation(out=kn8[0][:], in_=kn8[0][:], func=AF.Ln,
                                           scale=1.0 / 64.0),
             reads=[kn8[1]], writes=[kn8[1]])
        S.op("act", lambda e: e.activation(out=kn8[0][:], in_=kn8[0][:], func=AF.Exp,
                                           scale=-0.5),
             reads=[kn8[1]], writes=[kn8[1]], strict=True)
        for bb in range(4):
            jtg = t0 // 128 + bb
            bx, bxb = (ps_f[2], ps_fb[2]) if bb % 2 == 0 else (ps_f[3], ps_fb[3])
            S.op("pe", lambda e, bb=bb, bx=bx: e.transpose(
                out=bx[0:128, 0:8], in_=kn8[0][0:8, bb * 128:(bb + 1) * 128],
                identity=IDF[0:8, 0:8]),
                reads=[kn8[1], consts["ident_f32"][1]], writes=[bxb])
            S.op("dve", lambda e, jtg=jtg, bx=bx: e.tensor_copy(
                out=rk_all[0][:, jtg * 8:(jtg + 1) * 8], in_=bx[0:128, 0:8]),
                reads=[bxb], writes=[rk_all[1]])
        if l == 0 and w == 0:
            tap("rk0", rk_all[0][:, 0:32], [rk_all[1]])

    def phase2_A(l, w):
        t0 = w * WIN
        for i, nm in enumerate(("tq_a", "tq_b", "tk_a", "tk_b")):
            t, b = tabs[i]
            S.dma("sp", t[:], dr[nm][:, t0:t0 + 512], reads=[dbuf[nm]], writes=[b])
        yield
        grps = QK_GROUPS
        if "p2qk1" in stages:
            grps = QK_GROUPS[:1]
        if "p2qk2" in stages:
            grps = QK_GROUPS[:2]
        if "p2pre" in stages:
            grps = []
        for gi, grp in enumerate(grps):
            c0 = grp[0] * 128
            c1 = grp[-1] * 128 + 128
            for kc in range(NKC):
                slw, slwb = qkslab[kc % 4]
                S.dma("sp", slw[:, 0:c1 - c0],
                      dr["wqks"][l, kc * 128:(kc + 1) * 128, c0:c1],
                      reads=[dbuf["wqks"]], writes=[slwb])
                if l == 0 and w == 0 and gi == 0 and kc == 0:
                    tap("slab0", slw[:], [slwb])
                for gi2, m in enumerate(grp):
                    S.op("pe", lambda e, kc=kc, gi2=gi2, slw=slw, t0=t0: e.matmul(
                        out=ps_f[gi2][:], lhsT=slw[:, gi2 * 128:(gi2 + 1) * 128],
                        rhs=hv(kc, t0, 512), start=(kc == 0), stop=(kc == NKC - 1)),
                        reads=[slwb, ytb[kc]], writes=[ps_fb[gi2]])
            if l == 0 and w == 0 and gi == 0 and "psum0" in taps:
                S.op("dve", lambda e: e.tensor_copy(out=tmp1s[0][0][:], in_=ps_f[0][:]),
                     reads=[ps_fb[0]], writes=[tmp1s[0][1]])
                tap("psum0", tmp1s[0][0][:], [tmp1s[0][1]])
            if DEBUG_BARS in (1, "a"):
                S.barrier()
            for gi2, m in enumerate(grp):
                if "p2qkraw" in stages:
                    continue
                emit_norms_and_evac(l, m, w, ps_f[gi2], ps_fb[gi2])
                if DEBUG_BARS in (1, "b"):
                    S.barrier()
            if "p2qkraw" not in stages and grp and grp[-1] == 3:
                emit_q_chain(l, w)
            if "p2qkraw" not in stages and grp and grp[-1] == 7:
                emit_k_chain(l, w)
            yield
        for kc in (range(NKC) if not ("p2pre" in stages or "p2qk1" in stages or "p2qk2" in stages or "p2noV" in stages) else []):
            slv, slvb = vslab2[kc % 4]
            S.dma("sp", slv[:], dr["wqks"][l, kc * 128:(kc + 1) * 128, 1024:1536],
                  reads=[dbuf["wqks"]], writes=[slvb])
            for tt in range(4):
                S.op("pe", lambda e, kc=kc, tt=tt, slv=slv, t0=t0: e.matmul(
                    out=ps_f[tt][:], lhsT=hv(kc, t0 + tt * 128, 128), rhs=slv[:],
                    start=(kc == 0), stop=(kc == NKC - 1)),
                    reads=[slvb, ytb[kc]], writes=[ps_fb[tt]])
            if kc == 3:
                yield
        for tt in (range(4) if not ("p2pre" in stages or "p2qk1" in stages or "p2qk2" in stages or "p2noV" in stages) else []):
            slot = (w * 4 + tt) % 12
            t, b = v_aug[slot]
            S.op("dve", lambda e, t=t, tt=tt: e.tensor_copy(
                out=t[:].rearrange("p (g c) -> p g c", c=65)[:, :, 0:64],
                in_=ps_f[tt][:].rearrange("p (g c) -> p g c", c=64)),
                reads=[ps_fb[tt]], writes=[b])
            if l == 0 and w == 0 and tt == 0:
                tap("v0", t[:], [b])

    def phase2_B(l, w):
        t0 = w * WIN
        vpend = []
        vtail = None
        for h in range(HEADS):
            kt_h, pb = h // 2, (h % 2) * 64
            valid_jt = [jt for jt in range(8) if not (w == 0 and jt < 4)]
            vi = 0
            for jt in valid_jt:
                wslot = (w - 1 + jt // 4) % 3
                ktile, ktileb = kTs[wslot][kt_h]
                cols = (jt % 4) * 128
                a, b = JT_A[jt], JT_B[jt]
                bank, bankb = ps_f[5 + jt % 3], ps_fb[5 + jt % 3]
                S.op("pe", lambda e, ktile=ktile, cols=cols, a=a, b=b, bank=bank,
                     kt_h=kt_h, pb=pb, qt=qTt2[w % 2][kt_h]: e.matmul(
                         out=bank[:, 0:b - a], lhsT=ktile[pb:pb + 64, cols:cols + 128],
                         rhs=qt[0][pb:pb + 64, a:b], start=True, stop=True),
                     reads=[ktileb, qTt2[w % 2][kt_h][1]], writes=[bankb])
                if jt < 4:
                    reg0, msk = b - a - 128, "tri_past"
                else:
                    reg0, msk = 0, "tri_fut"
                S.op("dve", lambda e, bank=bank, reg0=reg0, msk=msk: e.tensor_tensor(
                    out=bank[:, reg0:reg0 + 128], in0=bank[:, reg0:reg0 + 128],
                    in1=consts[msk][0][:], op=ALU.min),
                    reads=[bankb, consts[msk][1]], writes=[bankb])
                if vi < len(vpend):
                    vpend[vi]()
                    vi += 1
                jtg = (w - 1) * 4 + jt
                S.op("act", lambda e, bank=bank, a=a, b=b, jt=jt, jtg=jtg, h=h:
                     e.activation(
                         out=pT[jt][0][:, a:b], in_=bank[:, 0:b - a], func=AF.Exp,
                         bias=neg_m[0][:], scale=rk_all[0][:, jtg * 8 + h:jtg * 8 + h + 1]),
                     reads=[bankb, neg_m[1], rk_all[1]], writes=[pT[jt][1]],
                     strict=True)
                if l == 0 and w == 0 and h == 0 and jt == 4:
                    tap("pT4", pT[4][0][:], [pT[4][1]])
            for f in vpend[vi:]:
                f()
            if vtail is not None:
                vtail()
                yield
            bank, bankb = ps_f[4], ps_fb[4]

            def mk_v(vslot, h, jt, first, last, bank=bank, bankb=bankb):
                def f():
                    S.op("pe", lambda e, vslot=vslot, h=h, jt=jt, bank=bank,
                         first=first, last=last: e.matmul(
                             out=bank[0:65, :],
                             lhsT=v_aug[vslot][0][:, h * 65:(h + 1) * 65],
                             rhs=pT[jt][0][:], start=first, stop=last),
                         reads=[v_aug[vslot][1], pT[jt][1]], writes=[bankb])
                return f
            vpend = [mk_v(((w - 1) * 4 + jt) % 12, h, jt, i == 0,
                          i == len(valid_jt) - 1)
                     for i, jt in enumerate(valid_jt)]
            def mk_tail(h=h, kt_h=kt_h, pb=pb, bank=bank, bankb=bankb, l=l, w=w):
              def tail():
                S.op("dve", lambda e, kt_h=kt_h, pb=pb, bank=bank: e.tensor_copy(
                    out=o_asm[kt_h][0][pb:pb + 64, :], in_=bank[0:64, :]),
                    reads=[bankb], writes=[o_asm[kt_h][1]])
                S.op("act", lambda e, hb=h % 2, bank=bank: e.activation(
                    out=rl[hb][0][:], in_=bank[64:65, :], func=AF.Ln),
                    reads=[bankb], writes=[rl[h % 2][1]])
                S.op("act", lambda e, hb=h % 2: e.activation(
                    out=rl[hb][0][:], in_=rl[hb][0][:], func=AF.Exp, scale=-1.0),
                    reads=[rl[h % 2][1]], writes=[rl[h % 2][1]], strict=True)
                if l == 0 and w == 0 and h == 0 and "oraw0" in taps:
                    tap("rl0", rl[0][0][:], [rl[0][1]])
                    S.op("dve", lambda e, bank=bank: e.tensor_copy(
                        out=tmp1s[0][0][:], in_=bank[0:65, :]),
                        reads=[bankb], writes=[tmp1s[0][1]])
                    tap("oraw0", tmp1s[0][0][:], [tmp1s[0][1]])
                if h % 2 == 1:
                    S.op("pe", lambda e, bank=bank: e.matmul(
                        out=bank[0:64, :], lhsT=consts["ones_row_f32"][0][0:1, 0:64],
                        rhs=rl[0][0][:], start=True, stop=True),
                        reads=[rl[0][1], consts["ones_row_f32"][1]], writes=[bankb])
                    S.op("pe", lambda e, bank=bank: e.matmul(
                        out=bank[64:128, :], lhsT=consts["ones_row_f32"][0][0:1, 0:64],
                        rhs=rl[1][0][:], start=True, stop=True, tile_position=(0, 64)),
                        reads=[rl[1][1], consts["ones_row_f32"][1]], writes=[bankb])
                    if l == 0 and w == 0 and h == 1 and "rbc_pair" in taps:
                        S.op("dve", lambda e, bank=bank: e.tensor_copy(
                            out=tmp2s[0][0][:], in_=bank[:]),
                            reads=[bankb], writes=[tmp2s[0][1]])
                        tap("rbc_pair", tmp2s[0][0][:], [tmp2s[0][1]])
                    S.op("dve", lambda e, kt_h=kt_h, bank=bank: e.tensor_mul(
                        out=o_asm[kt_h][0][:], in0=o_asm[kt_h][0][:], in1=bank[:]),
                        reads=[o_asm[kt_h][1], bankb], writes=[o_asm[kt_h][1]])
                    if l == 0 and w == 0 and h == 1:
                        tap("o01", o_asm[0][0][:], [o_asm[0][1]])
              return tail
            vtail = mk_tail()
        for f in vpend:
            f()
        if vtail is not None:
            vtail()
        yield
        for c0 in range(0, NKC, 2):
            pair = [(c, ps_f[5 + c % 3], ps_fb[5 + c % 3]) for c in (c0, c0 + 1)]
            for kt in range(4):
                for c, bank, bankb in pair:
                    S.op("pe", lambda e, kt=kt, c=c, bank=bank: e.matmul(
                        out=bank[:], lhsT=wo_sb[kt][0][:, c * 128:(c + 1) * 128],
                        rhs=o_asm[kt][0][:], start=(kt == 0), stop=(kt == 3)),
                        reads=[wo_sb[kt][1], o_asm[kt][1]], writes=[bankb])
            for c, bank, bankb in pair:
                S.op("dve", lambda e, c=c, t0=t0, bank=bank: e.tensor_add(
                    out=yt[c][:, 1 + t0:1 + t0 + 512], in0=yt[c][:, 1 + t0:1 + t0 + 512],
                    in1=bank[:]),
                    reads=[ytb[c], bankb], writes=[ytb[c]])

    def phase2_C_stats(l):
        for tc in range(NTC):
            t0 = tc * WIN
            for kc in range(NKC):
                cq, cqb = (csq, csq_b) if kc % 2 == 0 else (sq_sb[0], sq_sb[1])
                S.op("act", lambda e, kc=kc, t0=t0, cq=cq: e.activation(
                    out=cq[:], in_=hv(kc, t0, 512), func=AF.Square),
                    reads=[ytb[kc]], writes=[cqb])
                S.op("pe", lambda e, kc=kc, t0=t0, tc=tc: e.matmul(
                    out=ps_f[0][0:8, :], lhsT=consts["colsel8"][0][:, tc * 8:(tc + 1) * 8],
                    rhs=hv(kc, t0, 512), start=(tc == 0 and kc == 0),
                    stop=(tc == NTC - 1 and kc == NKC - 1)),
                    reads=[ytb[kc], consts["colsel8"][1]], writes=[ps_fb[0]])
                S.op("pe", lambda e, kc=kc, tc=tc, cq=cq: e.matmul(
                    out=ps_f[1][0:8, :], lhsT=consts["colsel8"][0][:, tc * 8:(tc + 1) * 8],
                    rhs=cq[:], start=(tc == 0 and kc == 0),
                    stop=(tc == NTC - 1 and kc == NKC - 1)),
                    reads=[cqb, consts["colsel8"][1]], writes=[ps_fb[1]])
        S.op("act", lambda e: e.mul(out=stat8a[0][:], in_=ps_f[0][0:8, :], mul=1.0 / D),
             reads=[ps_fb[0]], writes=[stat8a[1]])
        S.op("act", lambda e: e.activation(out=stat8b[0][:], in_=stat8a[0][:],
                                           func=AF.Square),
             reads=[stat8a[1]], writes=[stat8b[1]])
        S.op("dve", lambda e: e.scalar_tensor_tensor(
            out=stat8b[0][:], in0=ps_f[1][0:8, :], scalar=1.0 / D,
            in1=stat8b[0][:], op0=ALU.mult, op1=ALU.subtract),
            reads=[ps_fb[1], stat8b[1]], writes=[stat8b[1]], strict=True)
        S.op("act", lambda e: e.activation(out=stat8b[0][:], in_=stat8b[0][:],
                                           func=AF.Ln, bias=consts["eps_col"][0][0:8, :]),
             reads=[stat8b[1], consts["eps_col"][1]], writes=[stat8b[1]], strict=True)
        S.op("act", lambda e: e.activation(out=stat8b[0][:], in_=stat8b[0][:],
                                           func=AF.Exp, scale=-0.5),
             reads=[stat8b[1]], writes=[stat8b[1]], strict=True)
        S.op("dve", lambda e: e.tensor_mul(out=stat8a[0][:], in0=stat8a[0][:],
                                           in1=stat8b[0][:]),
             reads=[stat8a[1], stat8b[1]], writes=[stat8a[1]])
        S.op("act", lambda e: e.mul(out=stat8a[0][:], in_=stat8a[0][:], mul=-1.0),
             reads=[stat8a[1]], writes=[stat8a[1]], strict=True)

    def phase2_C(l, tc):
        t0 = tc * WIN
        S.op("pe", lambda e, tc=tc: e.matmul(
            out=ps_f[2][:], lhsT=consts["selrow8"][0][:, tc * 128:(tc + 1) * 128],
            rhs=stat8b[0][:], start=True, stop=True),
            reads=[stat8b[1], consts["selrow8"][1]], writes=[ps_fb[2]])
        S.op("act", lambda e: e.activation(out=a_bc[0][:], in_=ps_f[2][:], func=AF.Copy),
             reads=[ps_fb[2]], writes=[a_bc[1]])
        S.op("pe", lambda e, tc=tc: e.matmul(
            out=ps_f[3][:], lhsT=consts["selrow8"][0][:, tc * 128:(tc + 1) * 128],
            rhs=stat8a[0][:], start=True, stop=True),
            reads=[stat8a[1], consts["selrow8"][1]], writes=[ps_fb[3]])
        S.op("act", lambda e: e.activation(out=b_bc[0][:], in_=ps_f[3][:], func=AF.Copy),
             reads=[ps_fb[3]], writes=[b_bc[1]])
        for kc in range(NKC):
            tmpx = tmp1s[kc % 2]
            S.op("pool", lambda e, kc=kc, t0=t0, tmpx=tmpx: e.tensor_mul(
                out=tmpx[0][:], in0=hv(kc, t0, 512), in1=a_bc[0][:]),
                reads=[ytb[kc], a_bc[1]], writes=[tmpx[1]])
            S.op("pool", lambda e, tmpx=tmpx: e.tensor_add(
                out=tmpx[0][:], in0=tmpx[0][:], in1=b_bc[0][:]),
                reads=[tmpx[1], b_bc[1]], writes=[tmpx[1]])
            S.op("dve", lambda e, kc=kc, tmpx=tmpx: e.tensor_scalar(
                out=h2n[kc][0][:], in0=tmpx[0][:], scalar1=lg_col[0][:, kc:kc + 1],
                scalar2=lb_col[0][:, kc:kc + 1], op0=ALU.mult, op1=ALU.add),
                reads=[tmpx[1], lg_col[1], lb_col[1]], writes=[h2n[kc][1]], strict=True)
        for k in range(PAIRS):
            wk = min(128, HID - k * 128)
            sla, slab = w1sl[(k % 2) * 2]
            slg, slgb = w1sl[(k % 2) * 2 + 1]
            src = dr["w1s"][l].rearrange("(kc p) f -> p kc f", p=128)
            S.dma("sp", sla[:].rearrange("p (kc f) -> p kc f", kc=8)[:, :, 0:wk],
                  src[:, :, k * 128:k * 128 + wk],
                  reads=[dbuf["w1s"]], writes=[slab])
            S.dma("sp", slg[:].rearrange("p (kc f) -> p kc f", kc=8)[:, :, 0:wk],
                  src[:, :, HID + k * 128:HID + k * 128 + wk],
                  reads=[dbuf["w1s"]], writes=[slgb])
            pa, pab = ps_f[4 + (k % 2) * 2], ps_fb[4 + (k % 2) * 2]
            pg, pgb = ps_f[5 + (k % 2) * 2], ps_fb[5 + (k % 2) * 2]
            for kc in range(NKC):
                S.op("pe", lambda e, kc=kc, wk=wk, sla=sla, pa=pa: e.matmul(
                    out=pa[0:wk, :], lhsT=sla[:, kc * 128:kc * 128 + wk],
                    rhs=h2n[kc][0][:], start=(kc == 0), stop=(kc == NKC - 1)),
                    reads=[slab, h2n[kc][1]], writes=[pab])
                S.op("pe", lambda e, kc=kc, wk=wk, slg=slg, pg=pg: e.matmul(
                    out=pg[0:wk, :], lhsT=slg[:, kc * 128:kc * 128 + wk],
                    rhs=h2n[kc][0][:], start=(kc == 0), stop=(kc == NKC - 1)),
                    reads=[slgb, h2n[kc][1]], writes=[pgb])
            S.op("act", lambda e, wk=wk, k=k, pg=pg: e.activation(
                out=gel[0][0:wk, :], in_=pg[0:wk, :], func=AF.Gelu,
                bias=b1g_col[0][0:wk, k:k + 1]),
                reads=[pgb, b1g_col[1]], writes=[gel[1]], strict=True)
            S.op("dve", lambda e, wk=wk, k=k, pa=pa: e.scalar_tensor_tensor(
                out=ffn_sb[k][0][0:wk, :], in0=pa[0:wk, :],
                scalar=b1a_col[0][0:wk, k:k + 1], in1=gel[0][0:wk, :],
                op0=ALU.add, op1=ALU.mult),
                reads=[pab, gel[1], b1a_col[1]], writes=[ffn_sb[k][1]], strict=True)
        for fk in range(PAIRS):
            wk = min(128, HID - fk * 128)
            sl2, sl2b = w2sl[fk % 2]
            S.dma("sp", sl2[0:wk, :], dr["w2s"][l, fk * 128:fk * 128 + wk, :],
                  reads=[dbuf["w2s"]], writes=[sl2b])
            for c in range(NKC):
                S.op("pe", lambda e, fk=fk, c=c, wk=wk, sl2=sl2: e.matmul(
                    out=ps_f[c][:], lhsT=sl2[0:wk, c * 128:(c + 1) * 128],
                    rhs=ffn_sb[fk][0][0:wk, :], start=(fk == 0), stop=(fk == PAIRS - 1)),
                    reads=[sl2b, ffn_sb[fk][1]], writes=[ps_fb[c]])
        for c in range(NKC):
            S.op("dve", lambda e, c=c, t0=t0: e.scalar_tensor_tensor(
                out=yt[c][:, 1 + t0:1 + t0 + 512], in0=ps_f[c][:],
                scalar=b2_col[0][:, c:c + 1], in1=yt[c][:, 1 + t0:1 + t0 + 512],
                op0=ALU.add, op1=ALU.add),
                reads=[ps_fb[c], ytb[c], b2_col[1]], writes=[ytb[c]], strict=True)

    p2_small = any(s.startswith("p2") and s != "p2" for s in stages)
    p2_layers = range(2) if "p2" in stages else (range(1) if p2_small else [])
    for l in p2_layers:
        for kt in range(4):
            t, b = wo_sb[kt]
            S.dma("pool", t[:], dr["wo"][l, kt * 128:(kt + 1) * 128, :],
                  reads=[dbuf["wo"]], writes=[b])
        for (t, b), nm in [(qs_col, "qs_dup"), (ks_col, "ks_dup"), (qsn, "qs_nat"),
                           (ksn, "ks_nat"), (lg_col, "lg_cols"), (lb_col, "lb_cols"),
                           (b1a_col, "b1a_cols"), (b1g_col, "b1g_cols"),
                           (b2_col, "b2_cols")]:
            S.dma("pool", t[:], dr[nm][l], reads=[dbuf[nm]], writes=[b])
        S.op("dve", lambda e: e.reduce_max(out=mrow[0][:], in_=qsn[0][:], axis=AX.X,
                                           apply_absolute_value=True),
             reads=[qsn[1]], writes=[mrow[1]])
        S.op("dve", lambda e: e.reduce_max(out=mrow2[0][:], in_=ksn[0][:], axis=AX.X,
                                           apply_absolute_value=True),
             reads=[ksn[1]], writes=[mrow2[1]])
        S.op("dve", lambda e: e.tensor_mul(out=mrow[0][:], in0=mrow[0][:], in1=mrow2[0][:]),
             reads=[mrow[1], mrow2[1]], writes=[mrow[1]])
        S.op("act", lambda e: e.mul(out=mrow[0][:], in_=mrow[0][:], mul=-8.0),
             reads=[mrow[1]], writes=[mrow[1]], strict=True)
        S.op("pe", lambda e: e.matmul(
            out=ps_f[3][0:128, 0:1], lhsT=consts["ones_row_f32"][0][:], rhs=mrow[0][:],
            start=True, stop=True),
            reads=[mrow[1], consts["ones_row_f32"][1]], writes=[ps_fb[3]])
        S.op("act", lambda e: e.activation(out=neg_m[0][:], in_=ps_f[3][0:128, 0:1],
                                           func=AF.Copy),
             reads=[ps_fb[3]], writes=[neg_m[1]])
        for _ in phase2_A(l, 0):
            pass
        for w in (range(NW) if not p2_small else []):
            ga = phase2_A(l, w + 1) if w + 1 < NW else None
            gb = phase2_B(l, w)
            progressed = True
            while progressed:
                progressed = False
                try:
                    next(gb)
                    progressed = True
                except StopIteration:
                    pass
                if ga is not None:
                    try:
                        next(ga)
                        progressed = True
                    except StopIteration:
                        ga = None
        if p2_small and "p2B0" in stages:
            for _ in phase2_B(l, 0):
                pass
        if p2_small and not ("p2pre" in stages or "p2qk1" in stages or "p2qk2" in stages):
            pass
        if l == 0:
            tap("y_l0_attn", yt[0][:, 1:513], [ytb[0]])
        if not p2_small:
            phase2_C_stats(l)
        for tc in (reversed(range(NTC)) if not p2_small else []):
            phase2_C(l, tc)
        if l == 0:
            tap("y_l0", yt[0][:, 1:513], [ytb[0]])
    tap("y_l1", yt[0][:, 1:513], [ytb[0]])
    p2.close()
    S.barrier()

    p3 = ExitStack()
    rows3 = ExitStack()
    lnw_bc = sb("lnw_bc", [128, 1024], F32, p3)
    lnb_bc = sb("lnb_bc", [128, 1024], F32, p3)
    bv_bc = sb("bv_bc", [128, 1024], F32, p3)
    ps_r, ps_rb = psum("ps_r", [128, 512], F32, rows3)
    for nm, dst in ((("lnw_row", lnw_bc), ("lnb_row", lnb_bc), ("bqv_row", bv_bc))
                    if "p3" in stages else ()):
        t, b = sb(f"row_{nm}", [1, D], F32, rows3)
        S.dma("pool", t[:], dr[nm], reads=[dbuf[nm]], writes=[b])
        for half in range(2):
            S.op("pe", lambda e, t=t, half=half: e.matmul(
                out=ps_r[:], lhsT=consts["ones_row_f32"][0][:],
                rhs=t[:, half * 512:(half + 1) * 512], start=True, stop=True),
                reads=[b, consts["ones_row_f32"][1]], writes=[ps_rb])
            S.op("act", lambda e, dst=dst, half=half: e.activation(
                out=dst[0][:, half * 512:(half + 1) * 512], in_=ps_r[:], func=AF.Copy),
                reads=[ps_rb], writes=[dst[1]])
    rows3.close()
    S.barrier()

    for nm, shp, dt in [("mask_band", [128, 384], F32),
                        ("bqkv_cols", [128, 16], F32)]:
        t, b = sb("c_" + nm, shp, dt, p3)
        consts[nm] = (t, b)
        if "p3" in stages:
            S.dma("pool", t[:], dr[nm], reads=[dbuf[nm]], writes=[b])
    ps3, ps3b = [], []
    for i in range(7):
        t, b = psum(f"ps3_{i}", [128, 512], F32, p3)
        ps3.append(t)
        ps3b.append(b)
    ps_tb3, ps_tb3b = psum("ps_tb3", [128, 1024], BF16, p3)

    x2_sb = [sb(f"p3x2_{i}", [128, D], F32, p3) for i in range(4)]
    junk3 = sb("p3junk", [128, D], BF16, p3)
    xnT2 = [[sb(f"p3xnT{par}_{kc}", [128, 512], BF16, p3) for kc in range(NKC)]
            for par in range(2)]
    qT3 = [sb(f"p3qT{kc}", [128, 768], BF16, p3) for kc in range(NKC)]
    kT3 = [sb(f"p3kT{kc}", [128, 768], BF16, p3) for kc in range(NKC)]
    v3 = [sb(f"p3v{s}", [128, 1024], BF16, p3) for s in range(8)]
    wslab = [sb(f"p3ws{i}", [128, 512], BF16, p3) for i in range(8)]
    vslab = [sb(f"p3vs{i}", [128, 1024], BF16, p3) for i in range(3)]
    sT3 = [sb(f"p3sT{i}", [128, 384], BF16, p3) for i in range(4)]
    s_nat = [sb(f"p3sn{i}", [128, 1024], BF16, p3) for i in range(2)]
    t_half2 = [sb(f"p3th{i}", [128, 512], BF16, p3) for i in range(2)]
    tmp32 = [sb(f"p3tmp{i}", [128, 512], BF16, p3) for i in range(2)]
    g_sb2 = [sb(f"p3g{i}", [128, 1024], BF16, p3) for i in range(2)]
    sg_sb2 = [sb(f"p3sg{i}", [128, 1024], BF16, p3) for i in range(2)]
    out_sb = [sb(f"p3out{i}", [128, D], F32, p3) for i in range(2)]
    osq2 = [[sb(f"osq{j}_{i}", [128, 1], F32, p3) for i in range(2)] for j in range(2)]
    orow2 = [{nm: sb(f"p3o{nm}{j}", [128, 1], F32, p3)
              for nm in ("s1", "s1b", "mean", "m2", "var")} for j in range(2)]

    def p3_band_and_out(jj, tc):
        t_half = t_half2[jj % 2]
        tmp3 = tmp32[jj % 2]
        g_sb = g_sb2[jj % 2]
        sg_sb = sg_sb2[jj % 2]
        osq = osq2[jj % 2]
        orow = orow2[jj % 2]
        xt2, xt2b = x2_sb[jj % 4]
        S.dma("act", xt2[:], dr["x"][jj * 128:(jj + 1) * 128, :],
              reads=[dbuf["x"]], writes=[xt2b])
        W = min(384, NT - jj * 128)
        bank, bankb = ps3[4], ps3b[4]
        for bi in range((W + 127) // 128):
            wbi = min(128, W - bi * 128)
            ib = jj + bi
            for kc in range(NKC):
                S.op("pe", lambda e, kc=kc, jj=jj, bi=bi, ib=ib, wbi=wbi, bank=bank:
                     e.matmul(
                         out=bank[:, bi * 128:bi * 128 + wbi],
                         lhsT=kT3[kc][0][:, (jj % 6) * 128:(jj % 6) * 128 + 128],
                         rhs=qT3[kc][0][:, (ib % 6) * 128:(ib % 6) * 128 + wbi],
                         start=(kc == 0), stop=(kc == NKC - 1)),
                     reads=[kT3[kc][1], qT3[kc][1]], writes=[bankb])
        S.op("dve", lambda e, jj=jj, W=W, bank=bank: e.tensor_mul(
            out=sT3[jj % 4][0][:, 0:W], in0=bank[:, 0:W],
            in1=consts["mask_band"][0][:, 0:W]),
            reads=[bankb, consts["mask_band"][1]], writes=[sT3[jj % 4][1]])
        it = jj
        jts = [j for j in range(it - 2, it + 1) if j >= 0]
        for i, j2 in enumerate(jts):
            off = (it - j2) * 128
            for half in range(2):
                ob, obb = ps3[5 + half], ps3b[5 + half]
                S.op("pe", lambda e, j2=j2, off=off, half=half, ob=ob,
                     first=(i == 0), last=(i == len(jts) - 1): e.matmul(
                         out=ob[:], lhsT=sT3[j2 % 4][0][:, off:off + 128],
                         rhs=v3[j2 % 8][0][:, half * 512:(half + 1) * 512],
                         start=first, stop=last),
                     reads=[sT3[j2 % 4][1], v3[j2 % 8][1]], writes=[obb])
        for half in range(2):
            S.op("act", lambda e, half=half: e.activation(
                out=junk3[0][:, 0:512], in_=ps3[5 + half][:], func=AF.Square,
                accum_out=osq[half][0][:]),
                reads=[ps3b[5 + half]], writes=[osq[half][1], junk3[1]])
        S.op("dve", lambda e: e.reduce_sum(out=orow["s1"][0][:], in_=ps3[5][:], axis=AX.X),
             reads=[ps3b[5]], writes=[orow["s1"][1]])
        S.op("dve", lambda e: e.reduce_sum(out=orow["s1b"][0][:], in_=ps3[6][:], axis=AX.X),
             reads=[ps3b[6]], writes=[orow["s1b"][1]])
        S.op("dve", lambda e: e.tensor_add(out=orow["s1"][0][:], in0=orow["s1"][0][:],
                                           in1=orow["s1b"][0][:]),
             reads=[orow["s1"][1], orow["s1b"][1]], writes=[orow["s1"][1]])
        S.op("dve", lambda e: e.tensor_add(out=orow["s1b"][0][:], in0=osq[0][0][:],
                                           in1=osq[1][0][:]),
             reads=[osq[0][1], osq[1][1]], writes=[orow["s1b"][1]])
        S.op("act", lambda e: e.mul(out=orow["mean"][0][:], in_=orow["s1"][0][:],
                                    mul=1.0 / D),
             reads=[orow["s1"][1]], writes=[orow["mean"][1]], strict=True)
        S.op("act", lambda e: e.activation(out=orow["m2"][0][:], in_=orow["mean"][0][:],
                                           func=AF.Square),
             reads=[orow["mean"][1]], writes=[orow["m2"][1]], strict=True)
        S.op("dve", lambda e: e.scalar_tensor_tensor(
            out=orow["var"][0][:], in0=orow["s1b"][0][:], scalar=1.0 / D,
            in1=orow["m2"][0][:], op0=ALU.mult, op1=ALU.subtract),
            reads=[orow["s1b"][1], orow["m2"][1]], writes=[orow["var"][1]], strict=True)
        S.op("act", lambda e: e.activation(out=orow["var"][0][:], in_=orow["var"][0][:],
                                           func=AF.Sqrt, bias=consts["eps_col"][0][:]),
             reads=[orow["var"][1], consts["eps_col"][1]], writes=[orow["var"][1]], strict=True)
        S.op("dve", lambda e: e.reciprocal(out=orow["var"][0][:], in_=orow["var"][0][:]),
             reads=[orow["var"][1]], writes=[orow["var"][1]], strict=True)
        for half in range(2):
            S.op("dve", lambda e, half=half: e.tensor_scalar(
                out=t_half[0][:], in0=ps3[5 + half][:], scalar1=orow["mean"][0][:],
                scalar2=orow["var"][0][:], op0=ALU.subtract, op1=ALU.mult),
                reads=[ps3b[5 + half], orow["mean"][1], orow["var"][1]],
                writes=[t_half[1]], strict=True)
            S.op("pool", lambda e, half=half: e.tensor_mul(
                out=tmp3[0][:], in0=t_half[0][:],
                in1=lnw_bc[0][:, half * 512:(half + 1) * 512]),
                reads=[t_half[1], lnw_bc[1]], writes=[tmp3[1]])
            S.op("pool", lambda e, it=it, half=half: e.tensor_add(
                out=s_nat[it % 2][0][:, half * 512:(half + 1) * 512], in0=tmp3[0][:],
                in1=lnb_bc[0][:, half * 512:(half + 1) * 512]),
                reads=[tmp3[1], lnb_bc[1]], writes=[s_nat[it % 2][1]])
        for kc in range(NKC):
            S.op("pe", lambda e, kc=kc, it=it: e.transpose(
                out=ps_tb3[:, kc * 128:(kc + 1) * 128],
                in_=yt[kc][:, 1 + it * 128:1 + (it + 1) * 128], identity=IDB[:]),
                reads=[ytb[kc], consts["ident_bf"][1]], writes=[ps_tb3b])
        S.op("act", lambda e: e.activation(out=g_sb[0][:], in_=ps_tb3[:], func=AF.Silu),
             reads=[ps_tb3b], writes=[g_sb[1]])
        S.op("dve", lambda e, it=it: e.tensor_mul(
            out=sg_sb[0][:], in0=s_nat[it % 2][0][:], in1=g_sb[0][:]),
            reads=[s_nat[it % 2][1], g_sb[1]], writes=[sg_sb[1]])
        S.op("dve", lambda e, it=it: e.tensor_add(
            out=out_sb[it % 2][0][:], in0=x2_sb[it % 4][0][:], in1=sg_sb[0][:]),
            reads=[x2_sb[it % 4][1], sg_sb[1]], writes=[out_sb[it % 2][1]])
        S.dma("sp", dr["out"][it * 128:(it + 1) * 128, :], out_sb[it % 2][0][:],
              reads=[out_sb[it % 2][1]])
        if it == 0:
            tap("s_nat0", s_nat[0][0][:], [s_nat[0][1]])
            tap("g0", g_sb[0][:], [g_sb[1]])

    def p3_load_xnT(tc):
        t0 = tc * WIN
        xnT = xnT2[tc % 2]
        for kc in range(NKC):
            S.dma("sp" if kc % 2 else "act", xnT[kc][0][:],
                  dr["xnTd"][kc, :, t0:t0 + 512],
                  reads=[dbuf["xnTd"]], writes=[xnT[kc][1]])

    if "p3" in stages:
        p3_load_xnT(0)
    for tc in (range(NTC) if "p3" in stages else []):
        t0 = tc * WIN
        xnT = xnT2[tc % 2]
        for g in range(4):
            for kc in range(NKC):
                slw, slwb = wslab[kc]
                S.dma("sp", slw[:], dr["wqs"][kc * 128:(kc + 1) * 128,
                                              g * 512:(g + 1) * 512],
                      reads=[dbuf["wqs"]], writes=[slwb])
                for mi in range(4):
                    S.op("pe", lambda e, kc=kc, mi=mi, slw=slw, xnT=xnT: e.matmul(
                        out=ps3[mi][:], lhsT=slw[:, mi * 128:(mi + 1) * 128],
                        rhs=xnT[kc][0][:], start=(kc == 0), stop=(kc == NKC - 1)),
                        reads=[slwb, xnT[kc][1]], writes=[ps3b[mi]])
            for mi in range(4):
                m = g * 4 + mi
                dst, dstb = qT3[m] if m < 8 else kT3[m - 8]
                for q4 in range(4):
                    cb = (tc * 4 + q4) % 6
                    S.op("act", lambda e, dst=dst, cb=cb, q4=q4, mi=mi, m=m: e.activation(
                        out=dst[:, cb * 128:(cb + 1) * 128],
                        in_=ps3[mi][:, q4 * 128:(q4 + 1) * 128], func=AF.Identity,
                        bias=consts["bqkv_cols"][0][:, m:m + 1]),
                        reads=[ps3b[mi], consts["bqkv_cols"][1]], writes=[dstb],
                        strict=True)
        for wv in range(2):
            for kc in range(NKC):
                slv, slvb = vslab[kc % 3]
                S.dma("sp", slv[:], dr["wqs"][kc * 128:(kc + 1) * 128, 2048:3072],
                      reads=[dbuf["wqs"]], writes=[slvb])
                for tt in (wv * 2, wv * 2 + 1):
                    for half in range(2):
                        S.op("pe", lambda e, kc=kc, tt=tt, half=half, slv=slv, xnT=xnT: e.matmul(
                            out=ps3[(tt % 2) * 2 + half][:],
                            lhsT=xnT[kc][0][:, tt * 128:(tt + 1) * 128],
                            rhs=slv[:, half * 512:(half + 1) * 512],
                            start=(kc == 0), stop=(kc == NKC - 1)),
                            reads=[slvb, xnT[kc][1]],
                            writes=[ps3b[(tt % 2) * 2 + half]])
            for tt in (wv * 2, wv * 2 + 1):
                slot = (tc * 4 + tt) % 8
                for half in range(2):
                    S.op("dve", lambda e, slot=slot, tt=tt, half=half: e.tensor_add(
                        out=v3[slot][0][:, half * 512:(half + 1) * 512],
                        in0=ps3[(tt % 2) * 2 + half][:],
                        in1=bv_bc[0][:, half * 512:(half + 1) * 512]),
                        reads=[ps3b[(tt % 2) * 2 + half], bv_bc[1]],
                        writes=[v3[slot][1]])
        if tc + 1 < NTC:
            p3_load_xnT(tc + 1)
        jjs = [j for j in range(tc * 4 - 2, tc * 4 + 2) if 0 <= j < 32]
        if tc == NTC - 1:
            jjs += [30, 31]
        for jj in jjs:
            p3_band_and_out(jj, tc)

    p3.close()
    S.finish()

    block = es.enter_context(nc.Block())
    S.replay(block)
    es.close()
    return nc, tap_outs


_CACHE = {}


def _get_program(taps=()):
    key = tuple(sorted(taps))
    if key not in _CACHE:
        _CACHE[key] = build_program(taps)
    return _CACHE[key]


def kernel(**inputs):
    nc, _ = _get_program()
    consts = _host_consts()
    p = _pack_params(inputs)
    base = {}
    base.update(consts)
    for k, v in p.items():
        if k != "x_full":
            base[k] = v
    in_maps = []
    for c in range(8):
        m = dict(base)
        m["x"] = np.ascontiguousarray(p["x_full"][c])
        in_maps.append(m)
    res = run_bass_kernel_spmd(nc, in_maps, core_ids=list(range(8)))
    out = np.stack([np.asarray(r["out"]) for r in res.results], axis=0)
    return out.astype(np.float32)


# version 22
# speedup vs baseline: 1.0058x; 1.0058x over previous
import numpy as np
import ml_dtypes
import concourse.bass as bass
import concourse.mybir as mybir
from concourse.bass_utils import run_bass_kernel_spmd
from contextlib import ExitStack
import bisect

F32 = mybir.dt.float32
BF16 = mybir.dt.bfloat16
U8 = mybir.dt.uint8
AF = mybir.ActivationFunctionType
ALU = mybir.AluOpType
AX = mybir.AxisListType

NT = 4096
D = 1024
NKC = 8
WIN = 512
NW = NT // WIN
HEADS = 8
DH = 64
HID = 2730
PAIRS = 22
NTC = 8
BF = ml_dtypes.bfloat16

ENGINES = ("pe", "act", "dve", "pool", "sp")
QUEUES = ("sp", "act", "pool")


def _host_consts():
    c = {}
    c["ident_bf"] = np.eye(128, dtype=BF)
    c["ident_f32"] = np.eye(128, dtype=np.float32)
    P = np.zeros((128, 128), np.float32)
    for blk in (0, 64):
        for d in range(32):
            P[blk + d + 32, blk + d] = -1.0
            P[blk + d, blk + d + 32] = 1.0
    c["pmat"] = P.astype(BF)
    ones2 = np.zeros((128, 2), np.float32)
    ones2[0:64, 0] = 1.0
    ones2[64:128, 1] = 1.0
    c["ones2_f32"] = ones2
    c["ones2t_f32"] = np.ascontiguousarray(ones2.T)
    c["ones_row_f32"] = np.ones((1, 128), np.float32)
    o84 = np.zeros((128, 32), np.float32)
    for kt in range(4):
        for p in range(128):
            o84[p, kt * 8 + 2 * kt + p // 64] = 1.0
    c["ones8x4"] = o84.astype(BF)
    sq8 = np.zeros((8, 512), np.float32)
    for kt in range(4):
        for i in range(128):
            sq8[2 * kt + i // 64, kt * 128 + i] = 1.0
    c["selq8"] = sq8
    cs8 = np.zeros((128, 64), np.float32)
    for tc in range(8):
        cs8[:, tc * 8 + tc] = 1.0
    c["colsel8"] = cs8.astype(BF)
    sr8 = np.zeros((8, 1024), np.float32)
    for tc in range(8):
        sr8[tc, tc * 128:(tc + 1) * 128] = 1.0
    c["selrow8"] = sr8
    c["ones_col_bf"] = np.ones((128, 1), BF)
    half = np.arange(0, DH, 2, dtype=np.float64)
    inv_freq = 1.0 / 10000.0 ** (half / DH)
    t = np.arange(NT, dtype=np.float64)
    freqs = t[:, None] * inv_freq[None, :]
    freqs = np.concatenate([freqs, freqs], axis=-1)
    base = (half + 0.4 * DH) / (1.4 * DH)
    power = (t - NT // 2) / float(WIN // 2)
    scale = base[None, :] ** power[:, None]
    scale = np.concatenate([scale, scale], axis=-1)
    cos, sin = np.cos(freqs), np.sin(freqs)
    dup = lambda a: np.concatenate([a, a], axis=0).astype(np.float32)
    c["tq_a"] = dup((cos * scale).T)
    c["tq_b"] = dup((sin * scale).T)
    c["tk_a"] = dup((cos / scale).T)
    c["tk_b"] = dup((sin / scale).T)
    jj = np.arange(128)[:, None]
    ii = np.arange(384)[None, :]
    c["mask_band"] = (((jj <= ii) & (ii <= jj + 255)).astype(np.float32) / 512.0)
    r = np.arange(128)[None, :]
    jj1 = np.arange(128)[:, None]
    c["tri_past"] = np.where(jj1 >= r, 1.0e30, -1.0e30).astype(np.float32)
    c["tri_fut"] = np.where(jj1 <= r, 1.0e30, -1.0e30).astype(np.float32)
    c["eps_col"] = np.full((128, 1), 1.0e-5, np.float32)
    return c


def _pack_params(inputs):
    p = {}
    p["x_full"] = np.ascontiguousarray(inputs["L0_attn_x"], dtype=np.float32)
    p["W_qkv"] = np.ascontiguousarray(inputs["L0_attn_W_qkv"], dtype=np.float32)
    bq = np.asarray(inputs["L0_attn_b_qkv"], dtype=np.float32)
    p["bqkv_cols"] = np.ascontiguousarray(bq[0:2048].reshape(16, 128).T)
    p["bqv_row"] = np.ascontiguousarray(bq[2048:3072].reshape(1, 1024))
    p["lnw_row"] = np.asarray(inputs["L0_attn_ln_w"], np.float32).reshape(1, D)
    p["lnb_row"] = np.asarray(inputs["L0_attn_ln_b"], np.float32).reshape(1, D)
    p["wqkv"] = np.ascontiguousarray(inputs["L0_ffn_wqkv"], dtype=np.float32)
    p["wo"] = np.ascontiguousarray(inputs["L0_ffn_wo"], dtype=np.float32)
    p["w1"] = np.ascontiguousarray(inputs["L0_ffn_w1"], dtype=np.float32)
    p["w2"] = np.ascontiguousarray(inputs["L0_ffn_w2"], dtype=np.float32)
    qs = np.asarray(inputs["L0_ffn_q_scale"], np.float32)
    ks = np.asarray(inputs["L0_ffn_k_scale"], np.float32)
    p["qs_dup"] = np.ascontiguousarray(np.concatenate([qs, qs], 1).reshape(2, 128, 1))
    p["ks_dup"] = np.ascontiguousarray(np.concatenate([ks, ks], 1).reshape(2, 128, 1))
    p["qs_nat"] = qs.reshape(2, 1, 64).copy()
    p["ks_nat"] = ks.reshape(2, 1, 64).copy()
    lg = np.asarray(inputs["L0_ffn_ln_g"], np.float32)
    lb = np.asarray(inputs["L0_ffn_ln_b"], np.float32)
    p["lg_cols"] = np.ascontiguousarray(lg.reshape(2, 8, 128).transpose(0, 2, 1))
    p["lb_cols"] = np.ascontiguousarray(lb.reshape(2, 8, 128).transpose(0, 2, 1))
    b1 = np.asarray(inputs["L0_ffn_b1"], np.float32)
    b1a = np.zeros((2, 128, PAIRS), np.float32)
    b1g = np.zeros((2, 128, PAIRS), np.float32)
    for k in range(PAIRS):
        w = min(128, HID - k * 128)
        b1a[:, 0:w, k] = b1[:, k * 128:k * 128 + w]
        b1g[:, 0:w, k] = b1[:, HID + k * 128:HID + k * 128 + w]
    p["b1a_cols"] = b1a
    p["b1g_cols"] = b1g
    b2 = np.asarray(inputs["L0_ffn_b2"], np.float32)
    p["b2_cols"] = np.ascontiguousarray(b2.reshape(2, 8, 128).transpose(0, 2, 1))
    return p


class Buf:
    __slots__ = ("name", "writer", "readers", "psum")

    def __init__(self, name):
        self.name = name
        self.writer = None
        self.readers = []
        self.psum = False


class Step:
    __slots__ = ("eng", "fn", "waits", "incs", "idx", "qtok")

    def __init__(self, eng, fn, idx):
        self.eng = eng
        self.fn = fn
        self.waits = []
        self.incs = []
        self.idx = idx
        self.qtok = None


SERIAL = False
DEBUG_BARS = False
EN_UPTO = 99
PREFIX_N = None


class Sched:
    def __init__(self, nc, es):
        self.nc = nc
        self.es = es
        self.sem = {e: es.enter_context(nc.semaphore(f"sem_{e}")) for e in ENGINES}
        self.dsem = {}
        self.steps = {e: [] for e in ENGINES}
        self.cnt = {e: 0 for e in ENGINES}
        self.tick_idx = {e: [] for e in ENGINES}
        self.tick_val = {e: [] for e in ENGINES}
        self.hw = {e: {} for e in ENGINES}
        self.pending_bar = {e: [] for e in ENGINES}
        self.n_steps = 0

    def _get_dsem(self, key):
        if key not in self.dsem:
            sem = self.es.enter_context(self.nc.semaphore(f"dsem{len(self.dsem)}"))
            self.dsem[key] = [sem, 0]
        return self.dsem[key]

    def _step_token(self, st):
        if st.qtok is not None:
            return st.qtok
        e = st.eng
        i = bisect.bisect_left(self.tick_idx[e], st.idx)
        if i < len(self.tick_idx[e]):
            return ("e", e, self.tick_val[e][i])
        assert not st.incs
        self.cnt[e] += 1
        v = self.cnt[e]
        st.incs.append((self.sem[e], 1))
        self.tick_idx[e].append(st.idx)
        self.tick_val[e].append(v)
        return ("e", e, v)

    def _add_wait(self, step, tok, strict=False):
        if tok is None:
            return
        if isinstance(tok, Step):
            if tok.eng == step.eng and not strict and step.eng == "pe":
                return
            tok = self._step_token(tok)
        kind, key, val = tok
        if kind == "e" and key == step.eng and step.eng == "pe" and not strict:
            return
        hw = self.hw[step.eng]
        if hw.get((kind, key), -1) >= val:
            return
        hw[(kind, key)] = val
        sem = self.sem[key] if kind == "e" else self.dsem[key][0]
        step.waits.append((sem, val))

    def op(self, eng, fn, reads=(), writes=(), strict=False):
        st = Step(eng, fn, self.n_steps)
        self.n_steps += 1
        if SERIAL:
            for e2 in ENGINES:
                if e2 != eng:
                    for s2 in reversed(self.steps[e2]):
                        if s2.qtok is None:
                            self._add_wait(st, self._step_token(s2), strict=True)
                            break
            for key, (sem, cnt) in list(self.dsem.items()):
                if cnt:
                    self._add_wait(st, ("d", key, cnt), strict=True)
        for w in self.pending_bar[eng]:
            self._add_wait(st, w, strict=True)
        self.pending_bar[eng] = []
        for b in reads:
            self._add_wait(st, b.writer, strict=strict)
            if b.psum:
                for r in b.readers:
                    if isinstance(r, Step) and r.eng != eng:
                        self._add_wait(st, r, strict=True)
        for b in writes:
            self._add_wait(st, b.writer, strict=strict)
            for r in b.readers:
                self._add_wait(st, r, strict=strict)
        self.steps[eng].append(st)
        for b in writes:
            b.writer = st
            b.readers = []
        for b in reads:
            b.readers.append(st)
        return st

    def dma(self, q, out_ap, in_ap, reads=(), writes=(), sem_key=None, **kw):
        if sem_key is None:
            if writes:
                sem_key = writes[0]
            elif reads:
                sem_key = reads[0]
            else:
                sem_key = ("fq", q)
        ds = self._get_dsem(sem_key)
        ds[1] += 16
        tok = ("d", sem_key, ds[1])
        qs = ds[0]

        def fn(eng, out_ap=out_ap, in_ap=in_ap, kw=kw):
            return eng.dma_start(out=out_ap, in_=in_ap, **kw)

        st = self.op(q, fn, reads=reads, writes=writes)
        st.incs.append((qs, 16))
        st.qtok = tok
        for b in writes:
            b.writer = tok
        for b in reads:
            if b.readers and b.readers[-1] is st:
                b.readers[-1] = tok
            else:
                b.readers.append(tok)
        return tok

    def barrier(self):
        toks = []
        for e in ENGINES:
            for s2 in reversed(self.steps[e]):
                if s2.qtok is None:
                    toks.append(self._step_token(s2))
                    break
        for key, (sem, cnt) in self.dsem.items():
            if cnt:
                toks.append(("d", key, cnt))
        for e in ENGINES:
            self.pending_bar[e] = list(toks)

    def finish(self):
        for key, (sem, cnt) in self.dsem.items():
            if cnt:
                st = Step("sp", lambda eng, sem=sem, qv=cnt: eng.wait_ge(sem, qv),
                          self.n_steps)
                self.n_steps += 1
                self.steps["sp"].append(st)

    def replay(self, block):
        lim = PREFIX_N
        drain = {}
        if lim is not None:
            for e in ENGINES:
                for st in self.steps[e]:
                    if st.idx < lim and st.qtok is not None:
                        kind, key, val = st.qtok
                        drain[key] = max(drain.get(key, 0), val)

        def section(e):
            def run(eng):
                for st in self.steps[e]:
                    if lim is not None and st.idx >= lim:
                        continue
                    for (sem, val) in st.waits:
                        eng.wait_ge(sem, val)
                    inst = st.fn(eng)
                    for (sem, n) in st.incs:
                        inst.then_inc(sem, n)
                if e == "sp" and lim is not None:
                    for key, val in drain.items():
                        eng.wait_ge(self.dsem[key][0], val)
            return run
        block.tensor(section("pe"))
        block.scalar(section("act"))
        block.vector(section("dve"))
        block.gpsimd(section("pool"))
        block.sync(section("sp"))


def build_program(taps=(), stages=("p1", "p2", "p3")):
    taps = set(taps)
    stages = set(stages)
    nc = bass.Bass("TRN2", target_bir_lowering=False)
    es = ExitStack()
    tap_outs = []

    dr = {}
    dr["x"] = nc.dram_tensor("x", [NT, D], F32, kind="ExternalInput").ap()
    dr["out"] = nc.dram_tensor("out", [NT, D], F32, kind="ExternalOutput").ap()
    for nm, shp, dt in [
        ("W_qkv", [D, 3 * D], F32), ("bqkv_cols", [128, 16], F32),
        ("bqv_row", [1, D], F32), ("lnw_row", [1, D], F32), ("lnb_row", [1, D], F32),
        ("wqkv", [2, D, 1536], F32), ("wo", [2, 512, D], F32),
        ("w1", [2, D, 5460], F32), ("w2", [2, HID, D], F32),
        ("qs_dup", [2, 128, 1], F32), ("ks_dup", [2, 128, 1], F32),
        ("qs_nat", [2, 1, 64], F32), ("ks_nat", [2, 1, 64], F32),
        ("lg_cols", [2, 128, 8], F32), ("lb_cols", [2, 128, 8], F32),
        ("b1a_cols", [2, 128, PAIRS], F32), ("b1g_cols", [2, 128, PAIRS], F32),
        ("b2_cols", [2, 128, 8], F32),
        ("ident_bf", [128, 128], BF16), ("ident_f32", [128, 128], F32),
        ("pmat", [128, 128], BF16),
        ("ones2_f32", [128, 2], F32), ("ones2t_f32", [2, 128], F32),
        ("ones_row_f32", [1, 128], F32), ("ones_col_bf", [128, 1], BF16),
        ("ones8x4", [128, 32], BF16), ("selq8", [8, 512], F32),
        ("colsel8", [128, 64], BF16), ("selrow8", [8, 1024], F32),
        ("tq_a", [128, NT], F32), ("tq_b", [128, NT], F32),
        ("tk_a", [128, NT], F32), ("tk_b", [128, NT], F32),
        ("mask_band", [128, 384], F32), ("tri_past", [128, 128], F32),
        ("tri_fut", [128, 128], F32),
        ("eps_col", [128, 1], F32),
    ]:
        dr[nm] = nc.dram_tensor(nm, shp, dt, kind="ExternalInput").ap()
    dr["w1s"] = nc.dram_tensor("w1s", [2, D, 5460], BF16, kind="Internal").ap()
    dr["w2s"] = nc.dram_tensor("w2s", [2, HID, D], BF16, kind="Internal").ap()
    dr["wqs"] = nc.dram_tensor("wqs", [D, 3 * D], BF16, kind="Internal").ap()
    dr["xnTd"] = nc.dram_tensor("xnTd", [NKC, 128, NT], BF16, kind="Internal").ap()
    dr["wqks"] = nc.dram_tensor("wqks", [2, D, 1536], BF16, kind="Internal").ap()

    S = Sched(nc, es)

    def sb(name, shape, dtype, scope=es):
        t = scope.enter_context(nc.sbuf_tensor(name, shape, dtype))
        return t, Buf(name)

    def psum(name, shape, dtype, scope):
        t = scope.enter_context(nc.psum_tensor(name, shape, dtype))
        b = Buf(name)
        b.psum = True
        return t, b

    dbuf = {k: Buf(k) for k in dr}

    def tap(name, ap, reads):
        if name in taps:
            td = nc.dram_tensor(f"tap_{name}", list(ap.shape), ap.dtype,
                                kind="ExternalOutput").ap()
            S.dma("sp", td, ap, reads=reads, sem_key=("tap", name))
            tap_outs.append(f"tap_{name}")

    yt, ytb = [], []
    for kc in range(NKC):
        t, b = sb(f"y{kc}", [128, 1 + NT], BF16)
        yt.append(t)
        ytb.append(b)

    consts = {}
    for nm, shp, dt in [
        ("ident_bf", [128, 128], BF16), ("ident_f32", [128, 128], F32),
        ("pmat", [128, 128], BF16), ("ones2_f32", [128, 2], F32),
        ("ones2t_f32", [2, 128], F32), ("ones_row_f32", [1, 128], F32),
        ("ones_col_bf", [128, 1], BF16),
        ("tri_past", [128, 128], F32), ("tri_fut", [128, 128], F32),
        ("eps_col", [128, 1], F32),
        ("ones8x4", [128, 32], BF16), ("selq8", [8, 512], F32),
        ("colsel8", [128, 64], BF16), ("selrow8", [8, 1024], F32),
    ]:
        t, b = sb("c_" + nm, shp, dt)
        consts[nm] = (t, b)
        S.dma("pool", t[:], dr[nm], reads=[dbuf[nm]], writes=[b])

    IDB = consts["ident_bf"][0]
    IDF = consts["ident_f32"][0]

    for l in range(2):
        for kc in range(NKC):
            S.dma("pool", dr["wqks"][l, kc * 128:(kc + 1) * 128, :],
                  dr["wqkv"][l, kc * 128:(kc + 1) * 128, :],
                  reads=[dbuf["wqkv"]], sem_key="prep_wqks")
    for l in range(2):
        for kc in range(NKC):
            S.dma("pool", dr["w1s"][l, kc * 128:(kc + 1) * 128, :],
                  dr["w1"][l, kc * 128:(kc + 1) * 128, :],
                  reads=[dbuf["w1"]], sem_key="prep_w1s")
        for fk in range(6):
            a0, a1 = fk * 512, min(HID, (fk + 1) * 512)
            S.dma("pool", dr["w2s"][l, a0:a1, :], dr["w2"][l, a0:a1, :],
                  reads=[dbuf["w2"]], sem_key="prep_w2s")
    for kc in range(NKC):
        S.dma("pool", dr["wqs"][kc * 128:(kc + 1) * 128, :],
              dr["W_qkv"][kc * 128:(kc + 1) * 128, :],
              reads=[dbuf["W_qkv"]], sem_key="prep_wqs")
    for bname in ("w1s", "wqks", "w2s", "wqs"):
        key = {"w1s": "prep_w1s", "wqks": "prep_wqks", "w2s": "prep_w2s",
               "wqs": "prep_wqs"}[bname]
        dbuf[bname].writer = ("d", key, S.dsem[key][1])

    for kc in range(NKC):
        S.op("dve", lambda e, t=yt[kc]: e.memset(t[:, 0:1], 0.0), writes=[ytb[kc]])

    p1 = ExitStack()
    ps_trb, ps_trb_b = psum("ps_trb", [128, 1024], BF16, p1)
    x_sb = [sb(f"p1x{i}", [128, D], F32, p1) for i in range(4)]
    xn_sb = [sb(f"p1xn{i}", [128, D], BF16, p1) for i in range(4)]
    junk = sb("p1junk", [128, D], BF16, p1)
    stat = [{nm: sb(f"p1st{nm}{i}", [128, 1], F32, p1)
             for nm in ("s1", "s2", "mean", "m2", "var", "sd", "rstd")}
            for i in range(2)]

    def emit_ln_tile(xt, xtb, dst, dstb, stt, jt, jtb):
        S.op("dve", lambda e, a=xt, o=stt["s1"]: e.reduce_sum(
            out=o[0][:], in_=a[:], axis=AX.X),
            reads=[xtb], writes=[stt["s1"][1]])
        S.op("act", lambda e, a=xt, o=stt["s2"], j=jt: e.activation(
            out=j[:], in_=a[:], func=AF.Square, accum_out=o[0][:]),
            reads=[xtb], writes=[stt["s2"][1], jtb])
        S.op("act", lambda e, i=stt["s1"], o=stt["mean"]: e.mul(
            out=o[0][:], in_=i[0][:], mul=1.0 / D),
            reads=[stt["s1"][1]], writes=[stt["mean"][1]], strict=True)
        S.op("act", lambda e, i=stt["mean"], o=stt["m2"]: e.activation(
            out=o[0][:], in_=i[0][:], func=AF.Square),
            reads=[stt["mean"][1]], writes=[stt["m2"][1]], strict=True)
        S.op("dve", lambda e, i=stt["s2"], m2=stt["m2"], o=stt["var"]: e.scalar_tensor_tensor(
            out=o[0][:], in0=i[0][:], scalar=1.0 / D, in1=m2[0][:],
            op0=ALU.mult, op1=ALU.subtract),
            reads=[stt["s2"][1], stt["m2"][1]], writes=[stt["var"][1]], strict=True)
        S.op("act", lambda e, i=stt["var"], o=stt["sd"]: e.activation(
            out=o[0][:], in_=i[0][:], func=AF.Sqrt, bias=consts["eps_col"][0][:]),
            reads=[stt["var"][1], consts["eps_col"][1]], writes=[stt["sd"][1]], strict=True)
        S.op("dve", lambda e, i=stt["sd"], o=stt["rstd"]: e.reciprocal(
            out=o[0][:], in_=i[0][:]),
            reads=[stt["sd"][1]], writes=[stt["rstd"][1]], strict=True)
        S.op("dve", lambda e, a=xt, o=dst, m=stt["mean"], r=stt["rstd"]: e.tensor_scalar(
            out=o[:], in0=a[:], scalar1=m[0][:], scalar2=r[0][:], op0=ALU.subtract,
            op1=ALU.mult),
            reads=[xtb, stt["mean"][1], stt["rstd"][1]], writes=[dstb], strict=True)

    for it in (range(32) if "p1" in stages else []):
        sl = it % 4
        xt, xtb = x_sb[sl]
        S.dma("sp", xt[:], dr["x"][it * 128:(it + 1) * 128, :],
              reads=[dbuf["x"]], writes=[xtb])
        xnt, xnb = xn_sb[sl]
        emit_ln_tile(xt, xtb, xnt, xnb, stat[it % 2], junk[0], junk[1])
        for kc in range(NKC):
            S.op("pe", lambda e, src=xnt, kc=kc: e.transpose(
                out=ps_trb[:, kc * 128:(kc + 1) * 128],
                in_=src[:, kc * 128:(kc + 1) * 128], identity=IDB[:]),
                reads=[xnb, consts["ident_bf"][1]], writes=[ps_trb_b])
        for kc in range(NKC):
            S.op("dve", lambda e, kc=kc, it=it: e.tensor_copy(
                out=yt[kc][:, 1 + it * 128:1 + (it + 1) * 128],
                in_=ps_trb[:, kc * 128:(kc + 1) * 128]),
                reads=[ps_trb_b], writes=[ytb[kc]])
        if it == 0:
            tap("xn_t0", xn_sb[0][0][:], [xn_sb[0][1]])
    tap("y_p1", yt[0][:, 1:513], [ytb[0]])
    if "p1" in stages:
        for kc in range(NKC):
            S.dma("pool", dr["xnTd"][kc], yt[kc][:, 1:1 + NT],
                  reads=[ytb[kc]], sem_key="xnspill")
        dbuf["xnTd"].writer = ("d", "xnspill", S.dsem["xnspill"][1])
    p1.close()
    S.barrier()

    p2 = ExitStack()
    ps_f, ps_fb = [], []
    for i in range(8):
        t, b = psum(f"ps_f{i}", [128, 512], F32, p2)
        ps_f.append(t)
        ps_fb.append(b)

    wo_sb = [sb(f"wo{kt}", [128, 1024], BF16, p2) for kt in range(4)]
    qs_col = sb("qs_col", [128, 1], F32, p2)
    ks_col = sb("ks_col", [128, 1], F32, p2)
    qsn = sb("qsn", [1, 64], F32, p2)
    ksn = sb("ksn", [1, 64], F32, p2)
    lg_col = sb("lg_col", [128, 8], F32, p2)
    lb_col = sb("lb_col", [128, 8], F32, p2)
    b1a_col = sb("b1a_col", [128, PAIRS], F32, p2)
    b1g_col = sb("b1g_col", [128, PAIRS], F32, p2)
    b2_col = sb("b2_col", [128, 8], F32, p2)
    neg_m = sb("neg_m", [128, 1], F32, p2)
    mrow = sb("mrow", [1, 1], F32, p2)
    mrow2 = sb("mrow2", [1, 1], F32, p2)
    qkslab = [sb(f"qksl{i}", [128, 256], BF16, p2) for i in range(4)]
    vslab2 = [sb(f"vsl2{i}", [128, 512], BF16, p2) for i in range(4)]
    qTt2 = [[sb(f"qT{par}_{kt}", [128, 512], BF16, p2) for kt in range(4)]
            for par in range(2)]
    kTs = [[sb(f"kT{s}_{kt}", [128, 512], BF16, p2) for kt in range(4)] for s in range(3)]
    v_aug = [sb(f"vaug{s}", [128, 8 * 65], BF16, p2) for s in range(12)]
    sq_sb = sb("sq_sb", [128, 512], BF16, p2)
    qn8 = sb("qn8", [8, 512], F32, p2)
    kn8 = sb("kn8", [8, 512], F32, p2)
    rk_all = sb("rk_all", [128, 32 * 8], F32, p2)
    shufqs = [sb("shufq0", [128, 512], BF16, p2)] * 2
    tmp1s = [sb("tmp10", [128, 512], F32, p2)] * 2
    tmp2s = [sb("tmp20", [128, 512], F32, p2)] * 2
    tmp1 = tmp1s[0]
    tabs = [sb(f"tab{i}", [128, 512], F32, p2) for i in range(4)]
    pT = [sb(f"pT{jt}", [128, 512], BF16, p2) for jt in range(8)]
    o_asm = [sb(f"oasm{kt}", [128, 512], BF16, p2) for kt in range(4)]
    rl = [sb(f"rl{i}", [1, 512], F32, p2) for i in range(2)]
    h2n = [sb(f"h2n{kc}", [128, 512], BF16, p2) for kc in range(NKC)]
    csq, csq_b = sb("csq", [128, 512], BF16, p2)
    stat8a = sb("stat8a", [8, 512], F32, p2)
    stat8b = sb("stat8b", [8, 512], F32, p2)
    a_bc = sb("a_bc", [128, 512], F32, p2)
    b_bc = sb("b_bc", [128, 512], F32, p2)
    gel = sb("gel", [128, 512], BF16, p2)
    ffn_sb = [sb(f"ffn{k}", [128, 512], BF16, p2) for k in range(PAIRS)]
    w1sl = [sb(f"w1sl{i}", [128, 8 * 128], BF16, p2) for i in range(4)]
    w2sl = [sb(f"w2sl{i}", [128, 1024], BF16, p2) for i in range(2)]

    for s in range(12):
        t, b = v_aug[s]
        S.op("dve", lambda e, t=t: e.memset(
            t[:].rearrange("p (g c) -> p g c", c=65)[:, :, 64:65], 1.0), writes=[b])
    for jt in range(8):
        t, b = pT[jt]
        S.op("dve", lambda e, t=t: e.memset(t[:], 0.0), writes=[b])
    S.op("dve", lambda e: e.memset(rk_all[0][:], 0.0), writes=[rk_all[1]])

    def hv(kc, t0, n):
        if kc < 4:
            return yt[kc][:, 1 + t0:1 + t0 + n]
        return yt[kc][:, t0:t0 + n]

    JT_A = [max(0, jt * 128 - 512) for jt in range(8)]
    JT_B = [min(512, jt * 128 + 128) for jt in range(8)]
    QK_GROUPS = [(0, 1), (2, 3), (4, 5), (6, 7)]

    def emit_norms_and_evac(l, m, w, bank, bankb):
        t0 = w * WIN
        is_q = m < 4
        kt = m if is_q else m - 4
        shufq = shufqs[m % 2]
        tmp1 = tmp1s[m % 2]
        tmp2 = tmp2s[m % 2]
        S.op("act", lambda e, bank=bank: e.activation(
            out=sq_sb[0][:], in_=bank[:], func=AF.Square),
            reads=[bankb], writes=[sq_sb[1]])
        S.op("pe", lambda e, kt=kt: e.matmul(
            out=ps_f[2][0:8, :], lhsT=consts["ones8x4"][0][:, kt * 8:(kt + 1) * 8],
            rhs=sq_sb[0][:], start=(kt == 0), stop=(kt == 3)),
            reads=[sq_sb[1], consts["ones8x4"][1]], writes=[ps_fb[2]])
        if is_q:
            dst, dstb = qTt2[w % 2][kt]
            scl = qs_col
        else:
            dst, dstb = kTs[w % 3][kt]
            scl = ks_col
        S.op("act", lambda e, bank=bank, dst=dst, scl=scl: e.activation(
            out=dst[:], in_=bank[:], func=AF.Copy, scale=scl[0][:]),
            reads=[bankb, scl[1]], writes=[dstb], strict=True)
        if l == 0 and w == 0 and m == 0:
            tap("q0pre", dst[:], [dstb])
        ta, tab_ = tabs[0 if is_q else 2]
        tb, tbb = tabs[1 if is_q else 3]
        S.op("pe", lambda e, dst=dst, dstb=dstb: e.matmul(
            out=ps_f[3][:], lhsT=consts["pmat"][0][:], rhs=dst[:],
            start=True, stop=True),
            reads=[dstb, consts["pmat"][1]], writes=[ps_fb[3]])
        S.op("act", lambda e: e.activation(out=shufq[0][:], in_=ps_f[3][:], func=AF.Copy),
             reads=[ps_fb[3]], writes=[shufq[1]])
        S.op("pool", lambda e, dst=dst, ta=ta: e.tensor_mul(
            out=tmp1[0][:], in0=dst[:], in1=ta[:]),
            reads=[dstb, tab_], writes=[tmp1[1]])
        S.op("pool", lambda e, tb=tb: e.tensor_mul(
            out=tmp2[0][:], in0=shufq[0][:], in1=tb[:]),
            reads=[shufq[1], tbb], writes=[tmp2[1]])
        S.op("dve", lambda e, dst=dst: e.tensor_add(
            out=dst[:], in0=tmp1[0][:], in1=tmp2[0][:]),
            reads=[tmp1[1], tmp2[1]], writes=[dstb])
        if l == 0 and w == 0 and m == 4:
            tap("k0", dst[:], [dstb])

    def emit_q_chain(l, w):
        S.op("dve", lambda e: e.tensor_scalar_max(
            out=qn8[0][:], in0=ps_f[2][0:8, :], scalar1=1.0e-24),
            reads=[ps_fb[2]], writes=[qn8[1]])
        S.op("act", lambda e: e.activation(out=qn8[0][:], in_=qn8[0][:], func=AF.Ln),
             reads=[qn8[1]], writes=[qn8[1]])
        S.op("act", lambda e: e.activation(out=qn8[0][:], in_=qn8[0][:], func=AF.Exp,
                                           scale=-0.5),
             reads=[qn8[1]], writes=[qn8[1]], strict=True)
        for kt in range(4):
            bx, bxb = (ps_f[2], ps_fb[2]) if kt % 2 == 0 else (ps_f[3], ps_fb[3])
            S.op("pe", lambda e, kt=kt, bx=bx: e.matmul(
                out=bx[:], lhsT=consts["selq8"][0][:, kt * 128:(kt + 1) * 128],
                rhs=qn8[0][:], start=True, stop=True),
                reads=[qn8[1], consts["selq8"][1]], writes=[bxb])
            dst, dstb = qTt2[w % 2][kt]
            S.op("dve", lambda e, dst=dst, bx=bx: e.tensor_tensor(
                out=dst[:], in0=dst[:], in1=bx[:], op=ALU.mult),
                reads=[dstb, bxb], writes=[dstb])
            if l == 0 and w == 0 and kt == 0:
                tap("q0", dst[:], [dstb])

    def emit_k_chain(l, w):
        t0 = w * WIN
        S.op("dve", lambda e: e.tensor_scalar_max(
            out=kn8[0][:], in0=ps_f[2][0:8, :], scalar1=1.0e-24),
            reads=[ps_fb[2]], writes=[kn8[1]])
        S.op("act", lambda e: e.activation(out=kn8[0][:], in_=kn8[0][:], func=AF.Ln,
                                           scale=1.0 / 64.0),
             reads=[kn8[1]], writes=[kn8[1]])
        S.op("act", lambda e: e.activation(out=kn8[0][:], in_=kn8[0][:], func=AF.Exp,
                                           scale=-0.5),
             reads=[kn8[1]], writes=[kn8[1]], strict=True)
        for bb in range(4):
            jtg = t0 // 128 + bb
            bx, bxb = (ps_f[2], ps_fb[2]) if bb % 2 == 0 else (ps_f[3], ps_fb[3])
            S.op("pe", lambda e, bb=bb, bx=bx: e.transpose(
                out=bx[0:128, 0:8], in_=kn8[0][0:8, bb * 128:(bb + 1) * 128],
                identity=IDF[0:8, 0:8]),
                reads=[kn8[1], consts["ident_f32"][1]], writes=[bxb])
            S.op("dve", lambda e, jtg=jtg, bx=bx: e.tensor_copy(
                out=rk_all[0][:, jtg * 8:(jtg + 1) * 8], in_=bx[0:128, 0:8]),
                reads=[bxb], writes=[rk_all[1]])
        if l == 0 and w == 0:
            tap("rk0", rk_all[0][:, 0:32], [rk_all[1]])

    def phase2_A(l, w):
        t0 = w * WIN
        for i, nm in enumerate(("tq_a", "tq_b", "tk_a", "tk_b")):
            t, b = tabs[i]
            S.dma("act", t[:], dr[nm][:, t0:t0 + 512], reads=[dbuf[nm]], writes=[b])
        yield
        grps = QK_GROUPS
        if "p2qk1" in stages:
            grps = QK_GROUPS[:1]
        if "p2qk2" in stages:
            grps = QK_GROUPS[:2]
        if "p2pre" in stages:
            grps = []
        for gi, grp in enumerate(grps):
            c0 = grp[0] * 128
            c1 = grp[-1] * 128 + 128
            for kc in range(NKC):
                slw, slwb = qkslab[kc % 4]
                S.dma("sp", slw[:, 0:c1 - c0],
                      dr["wqks"][l, kc * 128:(kc + 1) * 128, c0:c1],
                      reads=[dbuf["wqks"]], writes=[slwb])
                if l == 0 and w == 0 and gi == 0 and kc == 0:
                    tap("slab0", slw[:], [slwb])
                for gi2, m in enumerate(grp):
                    S.op("pe", lambda e, kc=kc, gi2=gi2, slw=slw, t0=t0: e.matmul(
                        out=ps_f[gi2][:], lhsT=slw[:, gi2 * 128:(gi2 + 1) * 128],
                        rhs=hv(kc, t0, 512), start=(kc == 0), stop=(kc == NKC - 1)),
                        reads=[slwb, ytb[kc]], writes=[ps_fb[gi2]])
            if l == 0 and w == 0 and gi == 0 and "psum0" in taps:
                S.op("dve", lambda e: e.tensor_copy(out=tmp1s[0][0][:], in_=ps_f[0][:]),
                     reads=[ps_fb[0]], writes=[tmp1s[0][1]])
                tap("psum0", tmp1s[0][0][:], [tmp1s[0][1]])
            if DEBUG_BARS in (1, "a"):
                S.barrier()
            for gi2, m in enumerate(grp):
                if "p2qkraw" in stages:
                    continue
                emit_norms_and_evac(l, m, w, ps_f[gi2], ps_fb[gi2])
                if DEBUG_BARS in (1, "b"):
                    S.barrier()
            if "p2qkraw" not in stages and grp and grp[-1] == 3:
                emit_q_chain(l, w)
            if "p2qkraw" not in stages and grp and grp[-1] == 7:
                emit_k_chain(l, w)
            yield
        for kc in (range(NKC) if not ("p2pre" in stages or "p2qk1" in stages or "p2qk2" in stages or "p2noV" in stages) else []):
            slv, slvb = vslab2[kc % 4]
            S.dma("sp", slv[:], dr["wqks"][l, kc * 128:(kc + 1) * 128, 1024:1536],
                  reads=[dbuf["wqks"]], writes=[slvb])
            for tt in range(4):
                S.op("pe", lambda e, kc=kc, tt=tt, slv=slv, t0=t0: e.matmul(
                    out=ps_f[tt][:], lhsT=hv(kc, t0 + tt * 128, 128), rhs=slv[:],
                    start=(kc == 0), stop=(kc == NKC - 1)),
                    reads=[slvb, ytb[kc]], writes=[ps_fb[tt]])
            if kc == 3:
                yield
        for tt in (range(4) if not ("p2pre" in stages or "p2qk1" in stages or "p2qk2" in stages or "p2noV" in stages) else []):
            slot = (w * 4 + tt) % 12
            t, b = v_aug[slot]
            S.op("dve", lambda e, t=t, tt=tt: e.tensor_copy(
                out=t[:].rearrange("p (g c) -> p g c", c=65)[:, :, 0:64],
                in_=ps_f[tt][:].rearrange("p (g c) -> p g c", c=64)),
                reads=[ps_fb[tt]], writes=[b])
            if l == 0 and w == 0 and tt == 0:
                tap("v0", t[:], [b])

    def phase2_B(l, w):
        t0 = w * WIN
        vpend = []
        vtail = None
        for h in range(HEADS):
            kt_h, pb = h // 2, (h % 2) * 64
            valid_jt = [jt for jt in range(8) if not (w == 0 and jt < 4)]
            vi = 0
            for jt in valid_jt:
                wslot = (w - 1 + jt // 4) % 3
                ktile, ktileb = kTs[wslot][kt_h]
                cols = (jt % 4) * 128
                a, b = JT_A[jt], JT_B[jt]
                bank, bankb = ps_f[5 + jt % 3], ps_fb[5 + jt % 3]
                S.op("pe", lambda e, ktile=ktile, cols=cols, a=a, b=b, bank=bank,
                     kt_h=kt_h, pb=pb, qt=qTt2[w % 2][kt_h]: e.matmul(
                         out=bank[:, 0:b - a], lhsT=ktile[pb:pb + 64, cols:cols + 128],
                         rhs=qt[0][pb:pb + 64, a:b], start=True, stop=True),
                     reads=[ktileb, qTt2[w % 2][kt_h][1]], writes=[bankb])
                if jt < 4:
                    reg0, msk = b - a - 128, "tri_past"
                else:
                    reg0, msk = 0, "tri_fut"
                S.op("dve", lambda e, bank=bank, reg0=reg0, msk=msk: e.tensor_tensor(
                    out=bank[:, reg0:reg0 + 128], in0=bank[:, reg0:reg0 + 128],
                    in1=consts[msk][0][:], op=ALU.min),
                    reads=[bankb, consts[msk][1]], writes=[bankb])
                if vi < len(vpend):
                    vpend[vi]()
                    vi += 1
                jtg = (w - 1) * 4 + jt
                S.op("act", lambda e, bank=bank, a=a, b=b, jt=jt, jtg=jtg, h=h:
                     e.activation(
                         out=pT[jt][0][:, a:b], in_=bank[:, 0:b - a], func=AF.Exp,
                         bias=neg_m[0][:], scale=rk_all[0][:, jtg * 8 + h:jtg * 8 + h + 1]),
                     reads=[bankb, neg_m[1], rk_all[1]], writes=[pT[jt][1]],
                     strict=True)
                if l == 0 and w == 0 and h == 0 and jt == 4:
                    tap("pT4", pT[4][0][:], [pT[4][1]])
            for f in vpend[vi:]:
                f()
            if vtail is not None:
                vtail()
                yield
            bank, bankb = ps_f[4], ps_fb[4]

            def mk_v(vslot, h, jt, first, last, bank=bank, bankb=bankb):
                def f():
                    S.op("pe", lambda e, vslot=vslot, h=h, jt=jt, bank=bank,
                         first=first, last=last: e.matmul(
                             out=bank[0:65, :],
                             lhsT=v_aug[vslot][0][:, h * 65:(h + 1) * 65],
                             rhs=pT[jt][0][:], start=first, stop=last),
                         reads=[v_aug[vslot][1], pT[jt][1]], writes=[bankb])
                return f
            vpend = [mk_v(((w - 1) * 4 + jt) % 12, h, jt, i == 0,
                          i == len(valid_jt) - 1)
                     for i, jt in enumerate(valid_jt)]
            def mk_tail(h=h, kt_h=kt_h, pb=pb, bank=bank, bankb=bankb, l=l, w=w):
              def tail():
                S.op("dve", lambda e, kt_h=kt_h, pb=pb, bank=bank: e.tensor_copy(
                    out=o_asm[kt_h][0][pb:pb + 64, :], in_=bank[0:64, :]),
                    reads=[bankb], writes=[o_asm[kt_h][1]])
                S.op("act", lambda e, hb=h % 2, bank=bank: e.activation(
                    out=rl[hb][0][:], in_=bank[64:65, :], func=AF.Ln),
                    reads=[bankb], writes=[rl[h % 2][1]])
                S.op("act", lambda e, hb=h % 2: e.activation(
                    out=rl[hb][0][:], in_=rl[hb][0][:], func=AF.Exp, scale=-1.0),
                    reads=[rl[h % 2][1]], writes=[rl[h % 2][1]], strict=True)
                if l == 0 and w == 0 and h == 0 and "oraw0" in taps:
                    tap("rl0", rl[0][0][:], [rl[0][1]])
                    S.op("dve", lambda e, bank=bank: e.tensor_copy(
                        out=tmp1s[0][0][:], in_=bank[0:65, :]),
                        reads=[bankb], writes=[tmp1s[0][1]])
                    tap("oraw0", tmp1s[0][0][:], [tmp1s[0][1]])
                if h % 2 == 1:
                    S.op("pe", lambda e, bank=bank: e.matmul(
                        out=bank[0:64, :], lhsT=consts["ones_row_f32"][0][0:1, 0:64],
                        rhs=rl[0][0][:], start=True, stop=True),
                        reads=[rl[0][1], consts["ones_row_f32"][1]], writes=[bankb])
                    S.op("pe", lambda e, bank=bank: e.matmul(
                        out=bank[64:128, :], lhsT=consts["ones_row_f32"][0][0:1, 0:64],
                        rhs=rl[1][0][:], start=True, stop=True, tile_position=(0, 64)),
                        reads=[rl[1][1], consts["ones_row_f32"][1]], writes=[bankb])
                    if l == 0 and w == 0 and h == 1 and "rbc_pair" in taps:
                        S.op("dve", lambda e, bank=bank: e.tensor_copy(
                            out=tmp2s[0][0][:], in_=bank[:]),
                            reads=[bankb], writes=[tmp2s[0][1]])
                        tap("rbc_pair", tmp2s[0][0][:], [tmp2s[0][1]])
                    S.op("dve", lambda e, kt_h=kt_h, bank=bank: e.tensor_mul(
                        out=o_asm[kt_h][0][:], in0=o_asm[kt_h][0][:], in1=bank[:]),
                        reads=[o_asm[kt_h][1], bankb], writes=[o_asm[kt_h][1]])
                    if l == 0 and w == 0 and h == 1:
                        tap("o01", o_asm[0][0][:], [o_asm[0][1]])
              return tail
            vtail = mk_tail()
        for f in vpend:
            f()
        if vtail is not None:
            vtail()
        yield
        for c0 in range(0, NKC, 2):
            pair = [(c, ps_f[5 + c % 3], ps_fb[5 + c % 3]) for c in (c0, c0 + 1)]
            for kt in range(4):
                for c, bank, bankb in pair:
                    S.op("pe", lambda e, kt=kt, c=c, bank=bank: e.matmul(
                        out=bank[:], lhsT=wo_sb[kt][0][:, c * 128:(c + 1) * 128],
                        rhs=o_asm[kt][0][:], start=(kt == 0), stop=(kt == 3)),
                        reads=[wo_sb[kt][1], o_asm[kt][1]], writes=[bankb])
            for c, bank, bankb in pair:
                S.op("dve", lambda e, c=c, t0=t0, bank=bank: e.tensor_add(
                    out=yt[c][:, 1 + t0:1 + t0 + 512], in0=yt[c][:, 1 + t0:1 + t0 + 512],
                    in1=bank[:]),
                    reads=[ytb[c], bankb], writes=[ytb[c]])

    def phase2_C_stats(l):
        for tc in range(NTC):
            t0 = tc * WIN
            for kc in range(NKC):
                cq, cqb = (csq, csq_b) if kc % 2 == 0 else (sq_sb[0], sq_sb[1])
                S.op("act", lambda e, kc=kc, t0=t0, cq=cq: e.activation(
                    out=cq[:], in_=hv(kc, t0, 512), func=AF.Square),
                    reads=[ytb[kc]], writes=[cqb])
                S.op("pe", lambda e, kc=kc, t0=t0, tc=tc: e.matmul(
                    out=ps_f[0][0:8, :], lhsT=consts["colsel8"][0][:, tc * 8:(tc + 1) * 8],
                    rhs=hv(kc, t0, 512), start=(tc == 0 and kc == 0),
                    stop=(tc == NTC - 1 and kc == NKC - 1)),
                    reads=[ytb[kc], consts["colsel8"][1]], writes=[ps_fb[0]])
                S.op("pe", lambda e, kc=kc, tc=tc, cq=cq: e.matmul(
                    out=ps_f[1][0:8, :], lhsT=consts["colsel8"][0][:, tc * 8:(tc + 1) * 8],
                    rhs=cq[:], start=(tc == 0 and kc == 0),
                    stop=(tc == NTC - 1 and kc == NKC - 1)),
                    reads=[cqb, consts["colsel8"][1]], writes=[ps_fb[1]])
        S.op("act", lambda e: e.mul(out=stat8a[0][:], in_=ps_f[0][0:8, :], mul=1.0 / D),
             reads=[ps_fb[0]], writes=[stat8a[1]])
        S.op("act", lambda e: e.activation(out=stat8b[0][:], in_=stat8a[0][:],
                                           func=AF.Square),
             reads=[stat8a[1]], writes=[stat8b[1]])
        S.op("dve", lambda e: e.scalar_tensor_tensor(
            out=stat8b[0][:], in0=ps_f[1][0:8, :], scalar=1.0 / D,
            in1=stat8b[0][:], op0=ALU.mult, op1=ALU.subtract),
            reads=[ps_fb[1], stat8b[1]], writes=[stat8b[1]], strict=True)
        S.op("act", lambda e: e.activation(out=stat8b[0][:], in_=stat8b[0][:],
                                           func=AF.Ln, bias=consts["eps_col"][0][0:8, :]),
             reads=[stat8b[1], consts["eps_col"][1]], writes=[stat8b[1]], strict=True)
        S.op("act", lambda e: e.activation(out=stat8b[0][:], in_=stat8b[0][:],
                                           func=AF.Exp, scale=-0.5),
             reads=[stat8b[1]], writes=[stat8b[1]], strict=True)
        S.op("dve", lambda e: e.tensor_mul(out=stat8a[0][:], in0=stat8a[0][:],
                                           in1=stat8b[0][:]),
             reads=[stat8a[1], stat8b[1]], writes=[stat8a[1]])
        S.op("act", lambda e: e.mul(out=stat8a[0][:], in_=stat8a[0][:], mul=-1.0),
             reads=[stat8a[1]], writes=[stat8a[1]], strict=True)

    def phase2_C(l, tc):
        t0 = tc * WIN
        S.op("pe", lambda e, tc=tc: e.matmul(
            out=ps_f[2][:], lhsT=consts["selrow8"][0][:, tc * 128:(tc + 1) * 128],
            rhs=stat8b[0][:], start=True, stop=True),
            reads=[stat8b[1], consts["selrow8"][1]], writes=[ps_fb[2]])
        S.op("act", lambda e: e.activation(out=a_bc[0][:], in_=ps_f[2][:], func=AF.Copy),
             reads=[ps_fb[2]], writes=[a_bc[1]])
        S.op("pe", lambda e, tc=tc: e.matmul(
            out=ps_f[3][:], lhsT=consts["selrow8"][0][:, tc * 128:(tc + 1) * 128],
            rhs=stat8a[0][:], start=True, stop=True),
            reads=[stat8a[1], consts["selrow8"][1]], writes=[ps_fb[3]])
        S.op("act", lambda e: e.activation(out=b_bc[0][:], in_=ps_f[3][:], func=AF.Copy),
             reads=[ps_fb[3]], writes=[b_bc[1]])
        for kc in range(NKC):
            tmpx = tmp1s[kc % 2]
            S.op("pool", lambda e, kc=kc, t0=t0, tmpx=tmpx: e.tensor_mul(
                out=tmpx[0][:], in0=hv(kc, t0, 512), in1=a_bc[0][:]),
                reads=[ytb[kc], a_bc[1]], writes=[tmpx[1]])
            S.op("pool", lambda e, tmpx=tmpx: e.tensor_add(
                out=tmpx[0][:], in0=tmpx[0][:], in1=b_bc[0][:]),
                reads=[tmpx[1], b_bc[1]], writes=[tmpx[1]])
            S.op("dve", lambda e, kc=kc, tmpx=tmpx: e.tensor_scalar(
                out=h2n[kc][0][:], in0=tmpx[0][:], scalar1=lg_col[0][:, kc:kc + 1],
                scalar2=lb_col[0][:, kc:kc + 1], op0=ALU.mult, op1=ALU.add),
                reads=[tmpx[1], lg_col[1], lb_col[1]], writes=[h2n[kc][1]], strict=True)
        for k in range(PAIRS):
            wk = min(128, HID - k * 128)
            sla, slab = w1sl[(k % 2) * 2]
            slg, slgb = w1sl[(k % 2) * 2 + 1]
            src = dr["w1s"][l].rearrange("(kc p) f -> p kc f", p=128)
            S.dma("sp", sla[:].rearrange("p (kc f) -> p kc f", kc=8)[:, :, 0:wk],
                  src[:, :, k * 128:k * 128 + wk],
                  reads=[dbuf["w1s"]], writes=[slab])
            S.dma("sp", slg[:].rearrange("p (kc f) -> p kc f", kc=8)[:, :, 0:wk],
                  src[:, :, HID + k * 128:HID + k * 128 + wk],
                  reads=[dbuf["w1s"]], writes=[slgb])
            pa, pab = ps_f[4 + (k % 2) * 2], ps_fb[4 + (k % 2) * 2]
            pg, pgb = ps_f[5 + (k % 2) * 2], ps_fb[5 + (k % 2) * 2]
            for kc in range(NKC):
                S.op("pe", lambda e, kc=kc, wk=wk, sla=sla, pa=pa: e.matmul(
                    out=pa[0:wk, :], lhsT=sla[:, kc * 128:kc * 128 + wk],
                    rhs=h2n[kc][0][:], start=(kc == 0), stop=(kc == NKC - 1)),
                    reads=[slab, h2n[kc][1]], writes=[pab])
                S.op("pe", lambda e, kc=kc, wk=wk, slg=slg, pg=pg: e.matmul(
                    out=pg[0:wk, :], lhsT=slg[:, kc * 128:kc * 128 + wk],
                    rhs=h2n[kc][0][:], start=(kc == 0), stop=(kc == NKC - 1)),
                    reads=[slgb, h2n[kc][1]], writes=[pgb])
            S.op("act", lambda e, wk=wk, k=k, pg=pg: e.activation(
                out=gel[0][0:wk, :], in_=pg[0:wk, :], func=AF.Gelu,
                bias=b1g_col[0][0:wk, k:k + 1]),
                reads=[pgb, b1g_col[1]], writes=[gel[1]], strict=True)
            S.op("dve", lambda e, wk=wk, k=k, pa=pa: e.scalar_tensor_tensor(
                out=ffn_sb[k][0][0:wk, :], in0=pa[0:wk, :],
                scalar=b1a_col[0][0:wk, k:k + 1], in1=gel[0][0:wk, :],
                op0=ALU.add, op1=ALU.mult),
                reads=[pab, gel[1], b1a_col[1]], writes=[ffn_sb[k][1]], strict=True)
        for fk in range(PAIRS):
            wk = min(128, HID - fk * 128)
            sl2, sl2b = w2sl[fk % 2]
            S.dma("sp", sl2[0:wk, :], dr["w2s"][l, fk * 128:fk * 128 + wk, :],
                  reads=[dbuf["w2s"]], writes=[sl2b])
            for c in range(NKC):
                S.op("pe", lambda e, fk=fk, c=c, wk=wk, sl2=sl2: e.matmul(
                    out=ps_f[c][:], lhsT=sl2[0:wk, c * 128:(c + 1) * 128],
                    rhs=ffn_sb[fk][0][0:wk, :], start=(fk == 0), stop=(fk == PAIRS - 1)),
                    reads=[sl2b, ffn_sb[fk][1]], writes=[ps_fb[c]])
        for c in range(NKC):
            S.op("dve", lambda e, c=c, t0=t0: e.scalar_tensor_tensor(
                out=yt[c][:, 1 + t0:1 + t0 + 512], in0=ps_f[c][:],
                scalar=b2_col[0][:, c:c + 1], in1=yt[c][:, 1 + t0:1 + t0 + 512],
                op0=ALU.add, op1=ALU.add),
                reads=[ps_fb[c], ytb[c], b2_col[1]], writes=[ytb[c]], strict=True)

    p2_small = any(s.startswith("p2") and s != "p2" for s in stages)
    p2_layers = range(2) if "p2" in stages else (range(1) if p2_small else [])
    for l in p2_layers:
        for kt in range(4):
            t, b = wo_sb[kt]
            S.dma("pool", t[:], dr["wo"][l, kt * 128:(kt + 1) * 128, :],
                  reads=[dbuf["wo"]], writes=[b])
        for (t, b), nm in [(qs_col, "qs_dup"), (ks_col, "ks_dup"), (qsn, "qs_nat"),
                           (ksn, "ks_nat"), (lg_col, "lg_cols"), (lb_col, "lb_cols"),
                           (b1a_col, "b1a_cols"), (b1g_col, "b1g_cols"),
                           (b2_col, "b2_cols")]:
            S.dma("pool", t[:], dr[nm][l], reads=[dbuf[nm]], writes=[b])
        S.op("dve", lambda e: e.reduce_max(out=mrow[0][:], in_=qsn[0][:], axis=AX.X,
                                           apply_absolute_value=True),
             reads=[qsn[1]], writes=[mrow[1]])
        S.op("dve", lambda e: e.reduce_max(out=mrow2[0][:], in_=ksn[0][:], axis=AX.X,
                                           apply_absolute_value=True),
             reads=[ksn[1]], writes=[mrow2[1]])
        S.op("dve", lambda e: e.tensor_mul(out=mrow[0][:], in0=mrow[0][:], in1=mrow2[0][:]),
             reads=[mrow[1], mrow2[1]], writes=[mrow[1]])
        S.op("act", lambda e: e.mul(out=mrow[0][:], in_=mrow[0][:], mul=-8.0),
             reads=[mrow[1]], writes=[mrow[1]], strict=True)
        S.op("pe", lambda e: e.matmul(
            out=ps_f[3][0:128, 0:1], lhsT=consts["ones_row_f32"][0][:], rhs=mrow[0][:],
            start=True, stop=True),
            reads=[mrow[1], consts["ones_row_f32"][1]], writes=[ps_fb[3]])
        S.op("act", lambda e: e.activation(out=neg_m[0][:], in_=ps_f[3][0:128, 0:1],
                                           func=AF.Copy),
             reads=[ps_fb[3]], writes=[neg_m[1]])
        for _ in phase2_A(l, 0):
            pass
        for w in (range(NW) if not p2_small else []):
            ga = phase2_A(l, w + 1) if w + 1 < NW else None
            gb = phase2_B(l, w)
            progressed = True
            while progressed:
                progressed = False
                try:
                    next(gb)
                    progressed = True
                except StopIteration:
                    pass
                if ga is not None:
                    try:
                        next(ga)
                        progressed = True
                    except StopIteration:
                        ga = None
        if p2_small and "p2B0" in stages:
            for _ in phase2_B(l, 0):
                pass
        if p2_small and not ("p2pre" in stages or "p2qk1" in stages or "p2qk2" in stages):
            pass
        if l == 0:
            tap("y_l0_attn", yt[0][:, 1:513], [ytb[0]])
        if not p2_small:
            phase2_C_stats(l)
        for tc in (reversed(range(NTC)) if not p2_small else []):
            phase2_C(l, tc)
        if l == 0:
            tap("y_l0", yt[0][:, 1:513], [ytb[0]])
    tap("y_l1", yt[0][:, 1:513], [ytb[0]])
    p2.close()
    S.barrier()

    p3 = ExitStack()
    rows3 = ExitStack()
    lnw_bc = sb("lnw_bc", [128, 1024], F32, p3)
    lnb_bc = sb("lnb_bc", [128, 1024], F32, p3)
    bv_bc = sb("bv_bc", [128, 1024], F32, p3)
    ps_r, ps_rb = psum("ps_r", [128, 512], F32, rows3)
    for nm, dst in ((("lnw_row", lnw_bc), ("lnb_row", lnb_bc), ("bqv_row", bv_bc))
                    if "p3" in stages else ()):
        t, b = sb(f"row_{nm}", [1, D], F32, rows3)
        S.dma("pool", t[:], dr[nm], reads=[dbuf[nm]], writes=[b])
        for half in range(2):
            S.op("pe", lambda e, t=t, half=half: e.matmul(
                out=ps_r[:], lhsT=consts["ones_row_f32"][0][:],
                rhs=t[:, half * 512:(half + 1) * 512], start=True, stop=True),
                reads=[b, consts["ones_row_f32"][1]], writes=[ps_rb])
            S.op("act", lambda e, dst=dst, half=half: e.activation(
                out=dst[0][:, half * 512:(half + 1) * 512], in_=ps_r[:], func=AF.Copy),
                reads=[ps_rb], writes=[dst[1]])
    rows3.close()
    S.barrier()

    for nm, shp, dt in [("mask_band", [128, 384], F32),
                        ("bqkv_cols", [128, 16], F32)]:
        t, b = sb("c_" + nm, shp, dt, p3)
        consts[nm] = (t, b)
        if "p3" in stages:
            S.dma("pool", t[:], dr[nm], reads=[dbuf[nm]], writes=[b])
    ps3, ps3b = [], []
    for i in range(7):
        t, b = psum(f"ps3_{i}", [128, 512], F32, p3)
        ps3.append(t)
        ps3b.append(b)
    ps_tb3, ps_tb3b = psum("ps_tb3", [128, 1024], BF16, p3)

    x2_sb = [sb(f"p3x2_{i}", [128, D], F32, p3) for i in range(4)]
    junk3 = sb("p3junk", [128, D], BF16, p3)
    xnT2 = [[sb(f"p3xnT{par}_{kc}", [128, 512], BF16, p3) for kc in range(NKC)]
            for par in range(2)]
    qT3 = [sb(f"p3qT{kc}", [128, 768], BF16, p3) for kc in range(NKC)]
    kT3 = [sb(f"p3kT{kc}", [128, 768], BF16, p3) for kc in range(NKC)]
    v3 = [sb(f"p3v{s}", [128, 1024], BF16, p3) for s in range(8)]
    wslab = [sb(f"p3ws{i}", [128, 512], BF16, p3) for i in range(8)]
    vslab = [sb(f"p3vs{i}", [128, 1024], BF16, p3) for i in range(3)]
    sT3 = [sb(f"p3sT{i}", [128, 384], BF16, p3) for i in range(4)]
    s_nat = [sb(f"p3sn{i}", [128, 1024], BF16, p3) for i in range(2)]
    t_half2 = [sb(f"p3th{i}", [128, 512], BF16, p3) for i in range(2)]
    tmp32 = [sb(f"p3tmp{i}", [128, 512], BF16, p3) for i in range(2)]
    g_sb2 = [sb(f"p3g{i}", [128, 1024], BF16, p3) for i in range(2)]
    sg_sb2 = [sb(f"p3sg{i}", [128, 1024], BF16, p3) for i in range(2)]
    out_sb = [sb(f"p3out{i}", [128, D], F32, p3) for i in range(2)]
    osq2 = [[sb(f"osq{j}_{i}", [128, 1], F32, p3) for i in range(2)] for j in range(2)]
    orow2 = [{nm: sb(f"p3o{nm}{j}", [128, 1], F32, p3)
              for nm in ("s1", "s1b", "mean", "m2", "var")} for j in range(2)]

    def p3_band_and_out(jj, tc):
        t_half = t_half2[jj % 2]
        tmp3 = tmp32[jj % 2]
        g_sb = g_sb2[jj % 2]
        sg_sb = sg_sb2[jj % 2]
        osq = osq2[jj % 2]
        orow = orow2[jj % 2]
        xt2, xt2b = x2_sb[jj % 4]
        S.dma("act", xt2[:], dr["x"][jj * 128:(jj + 1) * 128, :],
              reads=[dbuf["x"]], writes=[xt2b])
        W = min(384, NT - jj * 128)
        bank, bankb = ps3[4], ps3b[4]
        for bi in range((W + 127) // 128):
            wbi = min(128, W - bi * 128)
            ib = jj + bi
            for kc in range(NKC):
                S.op("pe", lambda e, kc=kc, jj=jj, bi=bi, ib=ib, wbi=wbi, bank=bank:
                     e.matmul(
                         out=bank[:, bi * 128:bi * 128 + wbi],
                         lhsT=kT3[kc][0][:, (jj % 6) * 128:(jj % 6) * 128 + 128],
                         rhs=qT3[kc][0][:, (ib % 6) * 128:(ib % 6) * 128 + wbi],
                         start=(kc == 0), stop=(kc == NKC - 1)),
                     reads=[kT3[kc][1], qT3[kc][1]], writes=[bankb])
        S.op("dve", lambda e, jj=jj, W=W, bank=bank: e.tensor_mul(
            out=sT3[jj % 4][0][:, 0:W], in0=bank[:, 0:W],
            in1=consts["mask_band"][0][:, 0:W]),
            reads=[bankb, consts["mask_band"][1]], writes=[sT3[jj % 4][1]])
        it = jj
        jts = [j for j in range(it - 2, it + 1) if j >= 0]
        for i, j2 in enumerate(jts):
            off = (it - j2) * 128
            for half in range(2):
                ob, obb = ps3[5 + half], ps3b[5 + half]
                S.op("pe", lambda e, j2=j2, off=off, half=half, ob=ob,
                     first=(i == 0), last=(i == len(jts) - 1): e.matmul(
                         out=ob[:], lhsT=sT3[j2 % 4][0][:, off:off + 128],
                         rhs=v3[j2 % 8][0][:, half * 512:(half + 1) * 512],
                         start=first, stop=last),
                     reads=[sT3[j2 % 4][1], v3[j2 % 8][1]], writes=[obb])
        for half in range(2):
            S.op("act", lambda e, half=half: e.activation(
                out=junk3[0][:, 0:512], in_=ps3[5 + half][:], func=AF.Square,
                accum_out=osq[half][0][:]),
                reads=[ps3b[5 + half]], writes=[osq[half][1], junk3[1]])
        S.op("dve", lambda e: e.reduce_sum(out=orow["s1"][0][:], in_=ps3[5][:], axis=AX.X),
             reads=[ps3b[5]], writes=[orow["s1"][1]])
        S.op("dve", lambda e: e.reduce_sum(out=orow["s1b"][0][:], in_=ps3[6][:], axis=AX.X),
             reads=[ps3b[6]], writes=[orow["s1b"][1]])
        S.op("dve", lambda e: e.tensor_add(out=orow["s1"][0][:], in0=orow["s1"][0][:],
                                           in1=orow["s1b"][0][:]),
             reads=[orow["s1"][1], orow["s1b"][1]], writes=[orow["s1"][1]])
        S.op("dve", lambda e: e.tensor_add(out=orow["s1b"][0][:], in0=osq[0][0][:],
                                           in1=osq[1][0][:]),
             reads=[osq[0][1], osq[1][1]], writes=[orow["s1b"][1]])
        S.op("act", lambda e: e.mul(out=orow["mean"][0][:], in_=orow["s1"][0][:],
                                    mul=1.0 / D),
             reads=[orow["s1"][1]], writes=[orow["mean"][1]], strict=True)
        S.op("act", lambda e: e.activation(out=orow["m2"][0][:], in_=orow["mean"][0][:],
                                           func=AF.Square),
             reads=[orow["mean"][1]], writes=[orow["m2"][1]], strict=True)
        S.op("dve", lambda e: e.scalar_tensor_tensor(
            out=orow["var"][0][:], in0=orow["s1b"][0][:], scalar=1.0 / D,
            in1=orow["m2"][0][:], op0=ALU.mult, op1=ALU.subtract),
            reads=[orow["s1b"][1], orow["m2"][1]], writes=[orow["var"][1]], strict=True)
        S.op("act", lambda e: e.activation(out=orow["var"][0][:], in_=orow["var"][0][:],
                                           func=AF.Sqrt, bias=consts["eps_col"][0][:]),
             reads=[orow["var"][1], consts["eps_col"][1]], writes=[orow["var"][1]], strict=True)
        S.op("dve", lambda e: e.reciprocal(out=orow["var"][0][:], in_=orow["var"][0][:]),
             reads=[orow["var"][1]], writes=[orow["var"][1]], strict=True)
        for half in range(2):
            S.op("dve", lambda e, half=half: e.tensor_scalar(
                out=t_half[0][:], in0=ps3[5 + half][:], scalar1=orow["mean"][0][:],
                scalar2=orow["var"][0][:], op0=ALU.subtract, op1=ALU.mult),
                reads=[ps3b[5 + half], orow["mean"][1], orow["var"][1]],
                writes=[t_half[1]], strict=True)
            S.op("pool", lambda e, half=half: e.tensor_mul(
                out=tmp3[0][:], in0=t_half[0][:],
                in1=lnw_bc[0][:, half * 512:(half + 1) * 512]),
                reads=[t_half[1], lnw_bc[1]], writes=[tmp3[1]])
            S.op("pool", lambda e, it=it, half=half: e.tensor_add(
                out=s_nat[it % 2][0][:, half * 512:(half + 1) * 512], in0=tmp3[0][:],
                in1=lnb_bc[0][:, half * 512:(half + 1) * 512]),
                reads=[tmp3[1], lnb_bc[1]], writes=[s_nat[it % 2][1]])
        for kc in range(NKC):
            S.op("pe", lambda e, kc=kc, it=it: e.transpose(
                out=ps_tb3[:, kc * 128:(kc + 1) * 128],
                in_=yt[kc][:, 1 + it * 128:1 + (it + 1) * 128], identity=IDB[:]),
                reads=[ytb[kc], consts["ident_bf"][1]], writes=[ps_tb3b])
        S.op("act", lambda e: e.activation(out=g_sb[0][:], in_=ps_tb3[:], func=AF.Silu),
             reads=[ps_tb3b], writes=[g_sb[1]])
        S.op("dve", lambda e, it=it: e.tensor_mul(
            out=sg_sb[0][:], in0=s_nat[it % 2][0][:], in1=g_sb[0][:]),
            reads=[s_nat[it % 2][1], g_sb[1]], writes=[sg_sb[1]])
        S.op("dve", lambda e, it=it: e.tensor_add(
            out=out_sb[it % 2][0][:], in0=x2_sb[it % 4][0][:], in1=sg_sb[0][:]),
            reads=[x2_sb[it % 4][1], sg_sb[1]], writes=[out_sb[it % 2][1]])
        S.dma("sp", dr["out"][it * 128:(it + 1) * 128, :], out_sb[it % 2][0][:],
              reads=[out_sb[it % 2][1]])
        if it == 0:
            tap("s_nat0", s_nat[0][0][:], [s_nat[0][1]])
            tap("g0", g_sb[0][:], [g_sb[1]])

    def p3_load_xnT(tc):
        t0 = tc * WIN
        xnT = xnT2[tc % 2]
        for kc in range(NKC):
            S.dma("sp" if kc % 2 else "act", xnT[kc][0][:],
                  dr["xnTd"][kc, :, t0:t0 + 512],
                  reads=[dbuf["xnTd"]], writes=[xnT[kc][1]])

    if "p3" in stages:
        p3_load_xnT(0)
    for tc in (range(NTC) if "p3" in stages else []):
        t0 = tc * WIN
        xnT = xnT2[tc % 2]
        for g in range(4):
            for kc in range(NKC):
                slw, slwb = wslab[kc]
                S.dma("sp", slw[:], dr["wqs"][kc * 128:(kc + 1) * 128,
                                              g * 512:(g + 1) * 512],
                      reads=[dbuf["wqs"]], writes=[slwb])
                for mi in range(4):
                    S.op("pe", lambda e, kc=kc, mi=mi, slw=slw, xnT=xnT: e.matmul(
                        out=ps3[mi][:], lhsT=slw[:, mi * 128:(mi + 1) * 128],
                        rhs=xnT[kc][0][:], start=(kc == 0), stop=(kc == NKC - 1)),
                        reads=[slwb, xnT[kc][1]], writes=[ps3b[mi]])
            for mi in range(4):
                m = g * 4 + mi
                dst, dstb = qT3[m] if m < 8 else kT3[m - 8]
                for q4 in range(4):
                    cb = (tc * 4 + q4) % 6
                    S.op("act", lambda e, dst=dst, cb=cb, q4=q4, mi=mi, m=m: e.activation(
                        out=dst[:, cb * 128:(cb + 1) * 128],
                        in_=ps3[mi][:, q4 * 128:(q4 + 1) * 128], func=AF.Identity,
                        bias=consts["bqkv_cols"][0][:, m:m + 1]),
                        reads=[ps3b[mi], consts["bqkv_cols"][1]], writes=[dstb],
                        strict=True)
        for wv in range(2):
            for kc in range(NKC):
                slv, slvb = vslab[kc % 3]
                S.dma("sp", slv[:], dr["wqs"][kc * 128:(kc + 1) * 128, 2048:3072],
                      reads=[dbuf["wqs"]], writes=[slvb])
                for tt in (wv * 2, wv * 2 + 1):
                    for half in range(2):
                        S.op("pe", lambda e, kc=kc, tt=tt, half=half, slv=slv, xnT=xnT: e.matmul(
                            out=ps3[(tt % 2) * 2 + half][:],
                            lhsT=xnT[kc][0][:, tt * 128:(tt + 1) * 128],
                            rhs=slv[:, half * 512:(half + 1) * 512],
                            start=(kc == 0), stop=(kc == NKC - 1)),
                            reads=[slvb, xnT[kc][1]],
                            writes=[ps3b[(tt % 2) * 2 + half]])
            for tt in (wv * 2, wv * 2 + 1):
                slot = (tc * 4 + tt) % 8
                for half in range(2):
                    S.op("dve", lambda e, slot=slot, tt=tt, half=half: e.tensor_add(
                        out=v3[slot][0][:, half * 512:(half + 1) * 512],
                        in0=ps3[(tt % 2) * 2 + half][:],
                        in1=bv_bc[0][:, half * 512:(half + 1) * 512]),
                        reads=[ps3b[(tt % 2) * 2 + half], bv_bc[1]],
                        writes=[v3[slot][1]])
        if tc + 1 < NTC:
            p3_load_xnT(tc + 1)
        jjs = [j for j in range(tc * 4 - 2, tc * 4 + 2) if 0 <= j < 32]
        if tc == NTC - 1:
            jjs += [30, 31]
        for jj in jjs:
            p3_band_and_out(jj, tc)

    p3.close()
    S.finish()

    block = es.enter_context(nc.Block())
    S.replay(block)
    es.close()
    return nc, tap_outs


_CACHE = {}


def _get_program(taps=()):
    key = tuple(sorted(taps))
    if key not in _CACHE:
        _CACHE[key] = build_program(taps)
    return _CACHE[key]


def kernel(**inputs):
    nc, _ = _get_program()
    consts = _host_consts()
    p = _pack_params(inputs)
    base = {}
    base.update(consts)
    for k, v in p.items():
        if k != "x_full":
            base[k] = v
    in_maps = []
    for c in range(8):
        m = dict(base)
        m["x"] = np.ascontiguousarray(p["x_full"][c])
        in_maps.append(m)
    res = run_bass_kernel_spmd(nc, in_maps, core_ids=list(range(8)))
    out = np.stack([np.asarray(r["out"]) for r in res.results], axis=0)
    return out.astype(np.float32)
